# version 10
# baseline (speedup 1.0000x reference)
import os
import numpy as np
import ml_dtypes
import concourse.bass as bass
import concourse.mybir as mybir
from concourse.bass_utils import run_bass_kernel_spmd

F32 = mybir.dt.float32
BF16 = mybir.dt.bfloat16
U8 = mybir.dt.uint8
AF = mybir.ActivationFunctionType
ALU = mybir.AluOpType
AX = mybir.AxisListType

D = 1024
NT = 4096
TILES = 32
KT = 8
NSUB = 4
E = 15
LNEPS = 1e-5
ARENA_BYTES = 212608

ENGS = ("sync", "tensor", "vector", "scalar", "gpsimd")


class Sem:
    __slots__ = ("name", "n", "h", "dma")

    def __init__(self, name, dma=False):
        self.name, self.n, self.h, self.dma = name, 0, None, dma


class Prog:
    def __init__(self, nc):
        self.nc = nc
        self.ops = {e: [] for e in ENGS}
        self.sems = []
        self.hiwater = {e: {} for e in ENGS}
        self.auto = {}

    def sem(self, name, dma=False):
        s = Sem(name, dma)
        self.sems.append(s)
        return s

    def do(self, eng, fn, waits=(), inc=None, inc_amt=1):
        auto = self.auto.get(eng)
        if auto is not None and fn is not None:
            waits = list(waits) + [(auto, auto.n)]
            if inc is None:
                inc = auto
                inc_amt = 1
            else:
                assert inc is auto, (eng, inc.name)
        ws = []
        hw = self.hiwater[eng]
        for w in waits:
            if w is None:
                continue
            s, v = w
            assert v <= s.n, f"wait {s.name}>={v} but only {s.n} emitted"
            if hw.get(s.name, -1) >= v:
                continue
            hw[s.name] = v
            ws.append((s, v))
        after = None
        if inc is not None:
            inc.n += inc_amt
            after = inc.n
        self.ops[eng].append((ws, fn, inc, inc_amt))
        return after

    def barrier(self, b, engines=ENGS):
        for e in engines:
            auto = self.auto.get(e)
            if auto is not None:
                self.do(e, None, waits=[(auto, auto.n)])
            self.ops[e].append(([], (lambda eng: eng.nop()), b, 1))
            b.n += 1
        tgt = b.n
        for e in engines:
            self.do(e, None, waits=[(b, tgt)])

    def realize(self):
        from contextlib import ExitStack
        nc = self.nc
        with ExitStack() as st:
            for s in self.sems:
                s.h = st.enter_context(nc.semaphore(s.name))
            blk = st.enter_context(nc.Block())
            for e in ENGS:
                ops = self.ops[e]

                def body(eng, ops=ops):
                    for ws, fn, inc, amt in ops:
                        for s, v in ws:
                            eng.wait_ge(s.h, v)
                        if fn is not None:
                            inst = fn(eng)
                            if inc is not None:
                                inst.then_inc(inc.h, amt)
                getattr(blk, e)(body)


class DmaQ:

    def __init__(self, p, name, eng, k=4):
        self.p, self.eng, self.k = p, eng, k
        self.sems = [p.sem(f"{name}{i}") for i in range(k)]
        self.counts = [0] * k
        self.i = 0

    def issue(self, fn, waits=()):
        j = self.i % self.k
        self.i += 1
        s = self.sems[j]
        w = list(waits)
        if self.counts[j] > 0:
            w.append((s, 16 * self.counts[j]))
        self.counts[j] += 1
        self.p.do(self.eng, fn, waits=w, inc=s, inc_amt=16)
        return (s, 16 * self.counts[j])

    def drain_waits(self):
        return [(s, 16 * n) for s, n in zip(self.sems, self.counts) if n]


def _align(x, a=128):
    return (x + a - 1) & ~(a - 1)


class Ctx:
    pass


def build(nsub=NSUB):
    nc = bass.Bass("TRN2", target_bir_lowering=False,
                   detect_race_conditions=False)
    c = Ctx()
    c.nc = nc
    p = Prog(nc)
    c.p = p

    dt = nc.dram_tensor
    c.x_own = dt("x_own", [NT, D], F32, kind="ExternalInput").ap()
    c.x_pre = dt("x_pre", [NT, D], F32, kind="ExternalInput").ap()
    c.i_pmask = dt("pmask", [128, 1], F32, kind="ExternalInput").ap()
    c.i_trimask = dt("trimask", [128, 128], F32, kind="ExternalInput").ap()
    c.i_ident = dt("identb", [128, 128], BF16, kind="ExternalInput").ap()
    for n in ("aw_qT", "aw_kT", "aw_vT", "aw_oT", "bw_vT"):
        setattr(c, "i_" + n, dt(n, [KT, 128, D], F32, kind="ExternalInput").ap())
    c.i_aw_bq = dt("aw_bq", [128, KT], F32, kind="ExternalInput").ap()
    c.i_aw_bk = dt("aw_bk", [128, KT], F32, kind="ExternalInput").ap()
    c.i_rows = dt("rows", [1, 5 * D], F32, kind="ExternalInput").ap()
    c.i_gate = dt("gateS", [KT, 128, E], F32, kind="ExternalInput").ap()
    c.i_fc1S = dt("fc1S", [E, 8, 128, 2048], F32, kind="ExternalInput").ap()
    c.i_fc1b = dt("fc1bS", [128, E * 16], F32, kind="ExternalInput").ap()
    c.i_mult = dt("multS", [128, E * 8], F32, kind="ExternalInput").ap()
    c.i_fc2S = dt("fc2S", [E, 8, 128, 1024], F32, kind="ExternalInput").ap()
    c.i_fc2b = dt("fc2bS", [E, 1024], F32, kind="ExternalInput").ap()
    c.i_sfc1S = dt("sfc1S", [16, 128, 2048], F32, kind="ExternalInput").ap()
    c.i_sfc1b = dt("sfc1bS", [128, 32], F32, kind="ExternalInput").ap()
    c.i_smult = dt("smultS", [128, 16], F32, kind="ExternalInput").ap()
    c.i_sfc2S = dt("sfc2S", [16, 128, 1024], F32, kind="ExternalInput").ap()
    c.y_out = dt("y", [NT, D], F32, kind="ExternalOutput").ap()
    c.xs0 = dt("xs0", [NT, D], F32, kind="Internal").ap()
    c.xs1 = dt("xs1", [NT, D], F32, kind="Internal").ap()

    c.arena = nc.alloc_sbuf_tensor("arena", [128, ARENA_BYTES], U8)
    c.A32 = c.arena.bitcast(F32).ap()
    c.A16 = c.arena.bitcast(BF16).ap()

    def b32(off, parts, n):
        assert off % 4 == 0 and off + 4 * n <= ARENA_BYTES, (off, n)
        return c.A32[0:parts, off // 4: off // 4 + n]

    def b16(off, parts, n):
        assert off % 2 == 0 and off + 2 * n <= ARENA_BYTES, (off, n)
        return c.A16[0:parts, off // 2: off // 2 + n]
    c.b32, c.b16 = b32, b16

    off = 0

    def take(nbytes):
        nonlocal off
        o = off
        off = _align(off + nbytes)
        return o
    c.o_ident = take(128 * 2)
    c.o_trimask = take(128 * 4)
    c.o_pmask = take(64)
    c.o_rows = take(5 * D * 2)
    c.o_ones = take(128 * 2)
    c.o_gate = take(KT * E * 2)
    c.o_fc1b = take(E * 16 * 4)
    c.o_mult = take(E * 8 * 4)
    c.o_sfc1b = take(32 * 4)
    c.o_smult = take(16 * 4)
    c.o_fc2b = take(1024 * 2)
    c.o_awbq = take(KT * 4)
    c.o_awbk = take(KT * 4)
    c.o_eps = take(64)
    c.POOL = _align(off)
    assert c.POOL < 16 * 1024, c.POOL

    c.psum_h = nc.alloc_psum_tensor("psum", [128, 4096], F32)
    c.P32 = c.psum_h.ap()
    c.P16 = c.psum_h.bitcast(BF16).ap()

    c.bar = p.sem("bar")
    c.dq_x = DmaQ(p, "qx", "sync", 4)
    c.dq_st = DmaQ(p, "qs", "sync", 4)
    c.dq_wl = DmaQ(p, "qw", "gpsimd", 4)
    c.s_pe = [p.sem(f"s_pe{i}") for i in range(4)]
    _dv = p.sem("s_dv")
    _ac = p.sem("s_ac")
    c.s_dv = [_dv] * 4
    c.s_ac = [_ac] * 4
    p.auto["vector"] = _dv
    p.auto["scalar"] = _ac

    c.ident = b16(c.o_ident, 128, 128)
    c.trimask = b32(c.o_trimask, 128, 128)
    c.pmask = b32(c.o_pmask, 128, 1)
    c.ones = b16(c.o_ones, 1, 128)

    def row16(i, j0, n):
        base = c.o_rows // 2 + i * D
        return c.A16[0:1, base + j0: base + j0 + n]
    c.row16 = row16

    _preload(c)
    p.barrier(c.bar)

    if nsub == 1:
        _attn_a(c, c.x_own, c.y_out)
        p.barrier(c.bar)
    elif nsub == 2:
        _attn_a(c, c.x_own, c.xs0)
        p.barrier(c.bar)
        _ffn(c, c.xs0, c.y_out, fuse_attnb=False)
        p.barrier(c.bar)
    elif nsub == 3:
        _attn_a(c, c.x_own, c.xs0)
        p.barrier(c.bar)
        _ffn(c, c.xs0, c.y_out, fuse_attnb=True)
        p.barrier(c.bar)
    else:
        _attn_a(c, c.x_own, c.xs0)
        p.barrier(c.bar)
        _ffn(c, c.xs0, c.xs1, fuse_attnb=True)
        p.barrier(c.bar)
        _ffn(c, c.xs1, c.y_out, fuse_attnb=False)
        p.barrier(c.bar)

    p.do("sync", None, waits=c.dq_st.drain_waits())
    p.realize()
    return nc


def _preload(c):
    p = c.p

    def g(fn):
        return c.dq_wl.issue(fn)

    def s(fn):
        return c.dq_x.issue(fn)
    s(lambda e: e.dma_start(out=c.ident, in_=c.i_ident))
    s(lambda e: e.dma_start(out=c.trimask, in_=c.i_trimask))
    s(lambda e: e.dma_start(out=c.pmask, in_=c.i_pmask))
    s(lambda e: e.dma_start(out=c.b32(c.o_fc1b, 128, E * 16), in_=c.i_fc1b))
    s(lambda e: e.dma_start(out=c.b32(c.o_mult, 128, E * 8), in_=c.i_mult))
    s(lambda e: e.dma_start(out=c.b32(c.o_sfc1b, 128, 32), in_=c.i_sfc1b))
    s(lambda e: e.dma_start(out=c.b32(c.o_smult, 128, 16), in_=c.i_smult))
    s(lambda e: e.dma_start(out=c.b32(c.o_awbq, 128, KT), in_=c.i_aw_bq))
    s(lambda e: e.dma_start(out=c.b32(c.o_awbk, 128, KT), in_=c.i_aw_bk))
    g(lambda e: e.dma_start(
        out=c.A16[0:1, c.o_rows // 2: c.o_rows // 2 + 5 * D], in_=c.i_rows))
    g(lambda e: e.dma_start(out=c.b16(c.o_gate, 128, KT * E),
                            in_=c.i_gate.rearrange("k p n -> p k n")))
    g(lambda e: e.dma_start(
        out=c.A16[0:E, c.o_fc2b // 2: c.o_fc2b // 2 + 1024], in_=c.i_fc2b))
    p.do("vector", lambda e: e.memset(c.ones, 1.0), inc=c.s_dv[0])
    p.do("vector", lambda e: e.memset(c.b32(c.o_eps, 128, 1), LNEPS), inc=c.s_dv[0])
    c.nc.const_aps.register(F32, LNEPS, c.b32(c.o_eps, 128, 1))


LNB_BYTES = 8192 + 4096 + 192


def _new_lnb(base):
    return {"sq": base, "ytm": base + 8192, "st": base + 12288,
            "war": [None, None], "tr_war": [None, None]}


def _ln_tile(c, src32, ydst, ti, waits, TR, lnb):
    p = c.p
    pg = ti % 2
    sq = c.b32(lnb["sq"] + pg * 4096, 128, 1024)
    ytm = c.b16(lnb["ytm"] + pg * 2048, 128, 1024)
    st = c.b32(lnb["st"] + pg * 64, 128, 16)
    w0 = list(waits) + [lnb["war"][pg]]
    a_sq = p.do("scalar", lambda e: e.activation(
        sq, src32, AF.Square, accum_out=st[:, 1:2]), waits=w0, inc=c.s_ac[0])
    p.do("vector", lambda e: e.tensor_scalar(
        src32, src32, 1.0, 0.0, ALU.mult, ALU.add, accum_out=st[:, 0:1]),
        waits=w0)
    p.do("vector", lambda e: e.tensor_scalar(st[:, 2:3], st[:, 0:1], 1.0 / D, None, ALU.mult))
    p.do("vector", lambda e: e.tensor_scalar(st[:, 3:4], st[:, 1:2], 1.0 / D, None, ALU.mult),
         waits=[(c.s_ac[0], a_sq)])
    p.do("vector", lambda e: e.tensor_mul(st[:, 4:5], st[:, 2:3], st[:, 2:3]))
    v_var = p.do("vector", lambda e: e.tensor_tensor(st[:, 5:6], st[:, 3:4], st[:, 4:5], ALU.subtract),
                 inc=c.s_dv[0])
    p.do("scalar", lambda e: e.activation(st[:, 6:7], st[:, 5:6], AF.Ln, bias=LNEPS),
         waits=[(c.s_dv[0], v_var)])
    a_rstd = p.do("scalar", lambda e: e.activation(st[:, 7:8], st[:, 6:7], AF.Exp, scale=-0.5),
                  inc=c.s_ac[0])
    v_nmr = p.do("vector", lambda e: e.tensor_scalar(
        st[:, 8:9], st[:, 2:3], st[:, 7:8], -1.0, ALU.mult, ALU.mult),
        waits=[(c.s_ac[0], a_rstd)], inc=c.s_dv[0])
    a_ytm = p.do("scalar", lambda e: e.activation(
        ytm, src32, AF.Identity, scale=st[:, 7:8], bias=st[:, 8:9]),
        waits=[(c.s_dv[0], v_nmr)], inc=c.s_ac[0])
    last_ev = None
    for j in range(KT):
        sl = TR[j % 2]
        ws = [(c.s_ac[0], a_ytm), lnb["tr_war"][j % 2]]
        v_tr = p.do("tensor", lambda e, j=j, sl=sl: e.transpose(
            c.P16[:, sl:sl + 128], ytm[:, j * 128:(j + 1) * 128], c.ident),
            waits=ws, inc=c.s_pe[0])
        eng = "scalar" if j % 2 == 0 else "vector"
        sem = c.s_ac[0] if j % 2 == 0 else c.s_dv[0]
        if eng == "scalar":
            cp = p.do(eng, lambda e, j=j, sl=sl: e.copy(
                ydst(j), c.P16[:, sl:sl + 128]),
                waits=[(c.s_pe[0], v_tr)], inc=sem)
        else:
            cp = p.do(eng, lambda e, j=j, sl=sl: e.tensor_copy(
                ydst(j), c.P16[:, sl:sl + 128]),
                waits=[(c.s_pe[0], v_tr)], inc=sem)
        lnb["tr_war"][j % 2] = (sem, cp)
        last_ev = (sem, cp)
    lnb["war"][pg] = last_ev
    fin = p.do("scalar", lambda e: e.copy(st[:, 9:10], st[:, 0:1]),
               waits=[last_ev, lnb["tr_war"][0], lnb["tr_war"][1]],
               inc=c.s_ac[0])
    lnb["war"][pg] = (c.s_ac[0], fin)
    return fin


def _ffn(c, xin, xout, fuse_attnb=False):
    p = c.p
    B = c.POOL
    o_xacc = B
    o_y = o_xacc + 65536
    o_a = o_y + 32768
    o_w1 = o_a + 32768
    o_w2 = o_w1 + 8192
    o_comb = o_w2 + 32768
    o_comb16 = _align(o_comb + 960)
    o_combfm = _align(o_comb16 + 1024)
    o_gel = o_combfm + 4096
    o_gt = o_gel + 4096
    o_ge = o_gt + 128
    o_ln = _align(o_ge + 3 * 64)
    assert o_ln + LNB_BYTES <= ARENA_BYTES, o_ln

    c._gate = {"gt": o_gt, "e": o_ge, "comb": o_comb, "c16": o_comb16,
               "cfm": o_combfm,
               "war": {"ps": None, "buf": None, "ct": None}}
    lnb = _new_lnb(o_ln)
    env = dict(o_xacc=o_xacc, o_y=o_y, o_a=o_a, o_w1=o_w1, o_w2=o_w2,
               o_gel=o_gel, lnb=lnb, fuse=fuse_attnb,
               o_wvb=o_y, o_yb=o_y + 16384)
    if not hasattr(c, "_ffn_tr"):
        c._ffn_tr = {"w1_free": [None, None], "w2_free": [None, None],
                     "pairx_free": [None, None], "pairg_free": [None, None],
                     "gel_free": [None, None], "py_free": [None, None]}
    for half in range(2):
        _ffn_half(c, xin, xout, half, env)


F_PX = [0, 1024]
F_PG = [512, 1536]
F_PY = [2048, 3072]
F_TR = [6400, 7168]
F_PGATE = 2048
F_PCT = 2176


def _ffn_half(c, xin, xout, half, env):
    p = c.p
    base_t = half * 16
    s_pe, s_dv, s_ac = c.s_pe, c.s_dv, c.s_ac
    tr = c._ffn_tr
    o_xacc, o_y, o_a, o_w1, o_w2, o_gel = (env[k] for k in
                                           ("o_xacc", "o_y", "o_a", "o_w1", "o_w2", "o_gel"))
    lnb = env["lnb"]
    fuse = env["fuse"]
    o_wvb, o_yb = env["o_wvb"], env["o_yb"]

    xacc = lambda tt: c.b32(o_xacc + tt * 4096, 128, 1024)
    yfm = lambda k, c0, n: c.b16(o_y + k * 4096 + c0 * 2, 128, n)
    a_e = lambda j, c0, n: c.b16(o_a + j * 4096 + c0 * 2, 128, n)
    w1 = lambda sl, c0, n: c.b16(o_w1 + sl * 4096 + c0 * 2, 128, n)
    w2 = lambda sl, c0, n: c.b16(o_w2 + sl * 16384 + c0 * 2, 128, n)
    comb = lambda tt, ei: c.A32[:, (c._gate["comb"] // 4) + tt * 15 + ei:
                                (c._gate["comb"] // 4) + tt * 15 + ei + 1]
    combfm = lambda t0, n: c.A16[0:E, (c._gate["cfm"] // 2) + t0:
                                 (c._gate["cfm"] // 2) + t0 + n]
    gel = lambda sl: c.b32(o_gel + sl * 2048, 128, 512)

    ev_final = {}
    if half == 1:
        p.barrier(c.bar, engines=("sync", "tensor", "vector", "scalar"))

    ev_x, ev_y, ev_comb, ev_cfm = {}, {}, {}, {}
    for tt in range(16):
        gt = base_t + tt
        ev_x[tt] = c.dq_x.issue(lambda e, tt=tt, gt=gt: e.dma_start(
            out=xacc(tt), in_=xin[gt * 128:(gt + 1) * 128, :]))
    ev_c16 = {}
    for tt in range(16):
        ydst = lambda j, tt=tt: yfm(j, tt * 128, 128)
        ev_y[tt] = _ln_tile(c, xacc(tt), ydst, tt, [ev_x[tt]], F_TR, lnb)
        ev_comb[tt], ev_c16[tt] = _gate_tile(c, tt, yfm, comb, combfm, ev_y[tt])
    G2 = c._gate
    ctw = G2["war"]
    for tt in range(16):
        cb16 = c.b16(G2["c16"] + tt * 64, 128, 15)
        ptv = p.do("tensor", lambda e, cb16=cb16: e.transpose(
            c.P16[0:E, 2 * F_PCT: 2 * F_PCT + 128], cb16, c.ident),
            waits=[(s_dv[0], ev_c16[tt]), ctw["ct"]], inc=s_pe[3])
        acv = p.do("scalar", lambda e, tt=tt: e.copy(
            combfm(tt * 128, 128), c.P16[0:E, 2 * F_PCT: 2 * F_PCT + 128]),
            waits=[(s_pe[3], ptv)], inc=s_ac[0])
        ctw["ct"] = (s_ac[0], acv)
        ev_cfm[tt] = acv

    def fc1_like(nj, wsrc, bias_xy):
        ev_a_last = None
        ev_a_c = [None] * 4
        for j in range(nj):
            sl = j % 2
            lwv = c.dq_wl.issue(lambda e, j=j, sl=sl: e.dma_start(
                out=w1(sl, 0, 2048), in_=wsrc(j)),
                waits=[tr["w1_free"][sl]])
            last_pe = None
            for cc in range(4):
                pp = (j * 4 + cc) % 2
                bx, bg = bias_xy(j)
                wts = [lwv, tr["pairx_free"][pp],
                       (c.s_ac[0], ev_y[4 * cc + 3])]
                pe_x = None
                for k in range(KT):
                    pe_x = p.do("tensor", lambda e, k=k, pp=pp, sl=sl, cc=cc: e.matmul(
                        c.P32[:, F_PX[pp]:F_PX[pp] + 512],
                        w1(sl, k * 256, 128), yfm(k, cc * 512, 512),
                        start=(k == 0), stop=(k == KT - 1)),
                        waits=wts if k == 0 else [],
                        inc=s_pe[1] if k == KT - 1 else None)
                wts_g = [tr["pairg_free"][pp]]
                pe_g = None
                for k in range(KT):
                    pe_g = p.do("tensor", lambda e, k=k, pp=pp, sl=sl, cc=cc: e.matmul(
                        c.P32[:, F_PG[pp]:F_PG[pp] + 512],
                        w1(sl, k * 256 + 128, 128), yfm(k, cc * 512, 512),
                        start=(k == 0), stop=(k == KT - 1)),
                        waits=wts_g if k == 0 else [],
                        inc=s_pe[1] if k == KT - 1 else None)
                last_pe = pe_g
                ag = p.do("scalar", lambda e, pp=pp, bg=bg: e.activation(
                    gel(pp), c.P32[:, F_PG[pp]:F_PG[pp] + 512], AF.Gelu, bias=bg),
                    waits=[(s_pe[1], pe_g), tr["gel_free"][pp]], inc=s_ac[1])
                tr["pairg_free"][pp] = (s_ac[1], ag)
                av = p.do("vector", lambda e, j=j, cc=cc, pp=pp, bx=bx: e.scalar_tensor_tensor(
                    a_e(j, cc * 512, 512), c.P32[:, F_PX[pp]:F_PX[pp] + 512],
                    bx, gel(pp), ALU.add, ALU.mult),
                    waits=[(s_pe[1], pe_x), (s_ac[1], ag)], inc=s_dv[1])
                tr["pairx_free"][pp] = (s_dv[1], av)
                tr["gel_free"][pp] = (s_dv[1], av)
                ev_a_last = av
                ev_a_c[cc] = av
            tr["w1_free"][sl] = (s_pe[1], last_pe)
        return ev_a_c, (s_pe[1], s_pe[1].n)

    def fc2_like(jslots, ev_a_c, w2_ready, evac, has_tail=False):
        for tt in range(16):
            py = tt % 2
            wts = [tr["py_free"][py], (s_dv[1], ev_a_c[tt // 4]), w2_ready]
            pv = None
            for n in range(2):
                nj = len(jslots)
                for idx, (jj, sl, col0) in enumerate(jslots):
                    last = (not has_tail) and n == 1 and idx == nj - 1
                    pv2 = p.do("tensor", lambda e, tt=tt, n=n, jj=jj, sl=sl, col0=col0, py=py,
                               st=(idx == 0): e.matmul(
                        c.P32[:, F_PY[py] + n * 512: F_PY[py] + n * 512 + 512],
                        a_e(jj, tt * 128, 128), w2(sl, col0 + n * 512, 512),
                        start=st, stop=False),
                        waits=wts if (n == 0 and idx == 0) else [],
                        inc=s_pe[2] if last else None)
                    if last:
                        pv = pv2
            tr["py_free"][py] = evac(tt, py, pv)

    for ei in range(E):
        esl = ei % 2
        lw2 = None
        for k in range(KT):
            lw2 = c.dq_wl.issue(lambda e, ei=ei, k=k, esl=esl: e.dma_start(
                out=w2(esl, k * 1024, 1024), in_=c.i_fc2S[ei, k]),
                waits=[tr["w2_free"][esl]] if k == 0 else [])
        sc_ev = None
        for k in range(KT):
            sc_ev = p.do("vector", lambda e, ei=ei, k=k, esl=esl: e.tensor_scalar(
                w2(esl, k * 1024, 1024), w2(esl, k * 1024, 1024),
                c.A32[:, (c.o_mult // 4) + ei * 8 + k: (c.o_mult // 4) + ei * 8 + k + 1],
                None, ALU.mult),
                waits=[lw2], inc=s_dv[2])
        ev_a, _ = fc1_like(
            8, wsrc=lambda j, ei=ei: c.i_fc1S[ei, j],
            bias_xy=lambda j, ei=ei: (
                c.A32[:, (c.o_fc1b // 4) + ei * 16 + 2 * j: (c.o_fc1b // 4) + ei * 16 + 2 * j + 1],
                c.A32[:, (c.o_fc1b // 4) + ei * 16 + 2 * j + 1: (c.o_fc1b // 4) + ei * 16 + 2 * j + 2]))

        def evac_moe(tt, py, pv, ei=ei):
            dv = p.do("vector", lambda e, tt=tt, py=py, ei=ei: e.scalar_tensor_tensor(
                xacc(tt), c.P32[:, F_PY[py]:F_PY[py] + 1024],
                comb(tt, ei), xacc(tt), ALU.mult, ALU.add),
                waits=[(s_pe[2], pv), (s_dv[0], ev_comb[tt])], inc=s_dv[3])
            return (s_dv[3], dv)

        fc2_like([(j, esl, j * 1024) for j in range(8)], ev_a,
                 (s_dv[2], sc_ev), evac_moe)
        tr["w2_free"][esl] = (s_dv[3], s_dv[3].n)

    lw2 = None
    for sl in range(2):
        for k in range(KT):
            lw2 = c.dq_wl.issue(lambda e, sl=sl, k=k: e.dma_start(
                out=w2(sl, k * 1024, 1024), in_=c.i_sfc2S[sl * 8 + k]),
                waits=[tr["w2_free"][sl]] if k == 0 else [])
    sc_ev = None
    for sl in range(2):
        for k in range(KT):
            sc_ev = p.do("vector", lambda e, sl=sl, k=k: e.tensor_scalar(
                w2(sl, k * 1024, 1024), w2(sl, k * 1024, 1024),
                c.A32[:, (c.o_smult // 4) + sl * 8 + k: (c.o_smult // 4) + sl * 8 + k + 1],
                None, ALU.mult),
                waits=[lw2], inc=s_dv[2])
    for fh in range(2):
        ev_a, fc1_done = fc1_like(
            8, wsrc=lambda j, fh=fh: c.i_sfc1S[fh * 8 + j],
            bias_xy=lambda j, fh=fh: (
                c.A32[:, (c.o_sfc1b // 4) + 2 * (fh * 8 + j): (c.o_sfc1b // 4) + 2 * (fh * 8 + j) + 1],
                c.A32[:, (c.o_sfc1b // 4) + 2 * (fh * 8 + j) + 1: (c.o_sfc1b // 4) + 2 * (fh * 8 + j) + 2]))

        def evac_sh(tt, py, pv, fh=fh):
            if fh == 0:
                for n in range(2):
                    p.do("tensor", lambda e, py=py, n=n: e.matmul(
                        c.P32[:, F_PY[py] + n * 512:F_PY[py] + n * 512 + 512],
                        c.ones, c.row16(4, n * 512, 512),
                        start=False, stop=False))
                    pv = p.do("tensor", lambda e, tt=tt, py=py, n=n: e.matmul(
                        c.P32[:, F_PY[py] + n * 512:F_PY[py] + n * 512 + 512],
                        combfm(tt * 128, 128),
                        c.A16[0:E, c.o_fc2b // 2 + n * 512: c.o_fc2b // 2 + n * 512 + 512],
                        start=False, stop=(n == 1)),
                        waits=[(c.s_ac[0], ev_cfm[tt])] if n == 0 else [],
                        inc=s_pe[2] if n == 1 else None)
            dv = p.do("vector", lambda e, tt=tt, py=py: e.tensor_add(
                xacc(tt), xacc(tt), c.P32[:, F_PY[py]:F_PY[py] + 1024]),
                waits=[(s_pe[2], pv)], inc=s_dv[3])
            if fh == 1:
                if fuse:
                    ev_final[tt] = (s_dv[3], dv)
                else:
                    gt = base_t + tt
                    c.dq_st.issue(lambda e, tt=tt, gt=gt: e.dma_start(
                        out=xout[gt * 128:(gt + 1) * 128, :], in_=xacc(tt)),
                        waits=[(s_dv[3], dv)])
            return (s_dv[3], dv)

        if fuse and fh == 1:
            lwv = None
            for k in range(KT):
                lwv = c.dq_wl.issue(lambda e, k=k: e.dma_start(
                    out=c.b16(o_wvb + k * 2048, 128, 1024), in_=c.i_bw_vT[k]),
                    waits=[fc1_done] if k == 0 else [])
        fc2_like([(j, fh, j * 1024) for j in range(8)], ev_a,
                 (s_dv[2], sc_ev), evac_sh, has_tail=(fh == 0))
    for sl in range(2):
        tr["w2_free"][sl] = (s_dv[3], s_dv[3].n)

    if fuse:
        TRE = [0, 2048]
        PVE = [1024, 1536]
        epi_lnb = env.setdefault("epi_lnb", _new_lnb(env["lnb"]["sq"]))
        epi_tr = env.setdefault("epi_tr", {"pv_free": [None, None],
                                           "last_copy": None})
        for tt in range(16):
            sl = tt % 2
            ydst = lambda j, sl=sl: c.b16(o_yb + sl * 2048 + j * 256, 128, 128)
            evl = _ln_tile(c, xacc(tt), ydst, tt, [ev_final[tt]], TRE, epi_lnb)
            wts = [(c.s_ac[0], evl), lwv, epi_tr["pv_free"][sl]]
            pvv = None
            for n in range(2):
                for k in range(KT):
                    p.do("tensor", lambda e, k=k, n=n, sl=sl: e.matmul(
                        c.P32[:, PVE[sl] + 0:PVE[sl] + 512] if False else
                        c.P32[:, (PVE[0] if n == 0 else PVE[1]):(PVE[0] if n == 0 else PVE[1]) + 512],
                        c.b16(o_yb + sl * 2048 + k * 256, 128, 128),
                        c.b16(o_wvb + k * 2048 + n * 1024, 128, 512),
                        start=(k == 0), stop=False),
                        waits=wts if (k == 0 and n == 0) else [])
                pvv = p.do("tensor", lambda e, n=n: e.matmul(
                    c.P32[:, (PVE[0] if n == 0 else PVE[1]):(PVE[0] if n == 0 else PVE[1]) + 512],
                    c.ones, c.row16(3, n * 512, 512), start=False, stop=True),
                    inc=s_pe[3] if n == 1 else None)
            dvv = p.do("vector", lambda e, tt=tt: e.tensor_add(
                xacc(tt), xacc(tt), c.P32[:, PVE[0]:PVE[0] + 1024]),
                waits=[(s_pe[3], pvv)], inc=s_dv[3])
            epi_tr["pv_free"][0] = (s_dv[3], dvv)
            epi_tr["pv_free"][1] = (s_dv[3], dvv)
            gt = base_t + tt
            c.dq_st.issue(lambda e, tt=tt, gt=gt: e.dma_start(
                out=xout[gt * 128:(gt + 1) * 128, :], in_=xacc(tt)),
                waits=[(s_dv[3], dvv)])


def _gate_tile(c, tt, yfm, comb, combfm, ev_y):
    p = c.p
    s_pe, s_dv, s_ac = c.s_pe, c.s_dv, c.s_ac
    G = c._gate
    war = G["war"]
    gst = c.b32(G["gt"] + (tt % 2) * 64, 128, 16)
    et = c.b32(G["e"], 128, 15)
    et1 = c.b32(G["e"] + 64, 128, 15)
    km = c.b32(G["e"] + 128, 128, 15)
    gps = c.P32[:, F_PGATE:F_PGATE + 15]
    wts = [(c.s_ac[0], ev_y), war["ps"]]
    pv = None
    for k in range(KT):
        pv = p.do("tensor", lambda e, k=k: e.matmul(
            gps, yfm(k, tt * 128, 128),
            c.A16[:, (c.o_gate // 2) + k * 15: (c.o_gate // 2) + (k + 1) * 15],
            start=(k == 0), stop=(k == KT - 1)),
            waits=wts if k == 0 else [],
            inc=s_pe[3] if k == KT - 1 else None)
    p.do("vector", lambda e: e.reduce_max(gst[:, 0:1], gps, axis=AX.X),
         waits=[(s_pe[3], pv), war["buf"]])
    dv0 = p.do("vector", lambda e: e.tensor_scalar(
        gst[:, 1:2], gst[:, 0:1], -1.0, None, ALU.mult), inc=s_dv[0])
    av = p.do("scalar", lambda e: e.activation(et, gps, AF.Exp, bias=gst[:, 1:2]),
              waits=[(s_dv[0], dv0)], inc=s_ac[0])
    p.do("vector", lambda e: e.tensor_scalar(km, et, 1.0, None, ALU.is_ge),
         waits=[(s_ac[0], av)])
    p.do("vector", lambda e: e.scalar_tensor_tensor(et1, km, -2.0, et, ALU.mult, ALU.add))
    p.do("vector", lambda e: e.reduce_max(gst[:, 3:4], et1, axis=AX.X))
    p.do("vector", lambda e: e.tensor_scalar(km, et1, gst[:, 3:4], None, ALU.is_ge))
    p.do("vector", lambda e: e.scalar_tensor_tensor(et1, km, -2.0, et1, ALU.mult, ALU.add))
    p.do("vector", lambda e: e.reduce_max(gst[:, 4:5], et1, axis=AX.X))
    p.do("vector", lambda e: e.tensor_scalar(km, et, gst[:, 4:5], None, ALU.is_ge))
    p.do("vector", lambda e: e.tensor_mul(km, et, km))
    p.do("vector", lambda e: e.reduce_sum(gst[:, 5:6], km, axis=AX.X))
    p.do("vector", lambda e: e.tensor_scalar(gst[:, 6:7], gst[:, 5:6], 1e-20, None, ALU.add))
    p.do("vector", lambda e: e.reciprocal(gst[:, 7:8], gst[:, 6:7]))
    dvc = p.do("vector", lambda e, tt=tt: e.tensor_scalar(
        c.A32[:, (G["comb"] // 4) + tt * 15: (G["comb"] // 4) + (tt + 1) * 15],
        km, gst[:, 7:8], None, ALU.mult), inc=s_dv[0])
    cb16 = c.b16(G["c16"] + tt * 64, 128, 15)
    dv16 = p.do("vector", lambda e, tt=tt: e.tensor_copy(
        cb16, c.A32[:, (G["comb"] // 4) + tt * 15: (G["comb"] // 4) + (tt + 1) * 15]),
        inc=s_dv[0])
    war["buf"] = (s_dv[0], dv16)
    war["ps"] = (s_dv[0], dv16)
    return dvc, dv16


A_PA = [0, 512]
A_PN = [1024, 1536]
A_PS = [2048, 2560]
A_PP = [1024, 1536]
A_PR = [1024, 1536]
A_TRLN = [6400, 7168]
A_TRO = [0, 1024]


def _attn_a(c, xin, xout):
    p = c.p
    B = c.POOL
    o_w = B
    o_y = o_w + 4 * 16384
    o_q = o_y + 16384
    o_kfm = o_q + 16384
    o_ktm = o_kfm + 16384
    o_vp = o_ktm + 16384
    o_out = o_y
    o_S32 = _align(o_vp + 16640)
    o_S16 = _align(o_S32 + 4160)
    o_atm = _align(o_S16 + 2080)
    o_sp = _align(o_atm + 512)
    o_rec = o_sp + 4096
    o_ofm = _align(o_rec + 128)
    o_x2 = _align(o_ofm + 4096)
    o_nx = o_x2 + 8192
    o_ln = o_nx + 8192
    assert o_ln + LNB_BYTES <= ARENA_BYTES, o_ln

    lnb = _new_lnb(o_ln)
    globals()["ATTN_OFFS"] = dict(o_w=o_w, o_y=o_y, o_q=o_q,
        o_kfm=o_kfm, o_ktm=o_ktm, o_vp=o_vp, o_out=o_out, o_S32=o_S32,
        o_S16=o_S16, o_x2=o_x2, o_nx=o_nx, o_ofm=o_ofm)
    wmat = lambda wi, k, c0, n: c.b16(o_w + wi * 16384 + k * 2048 + c0 * 2, 128, n)
    yq = lambda k, c0, n: c.b16(o_y + k * 2048 + c0 * 2, 128, n)

    lw = {}
    for wi, (name, src) in enumerate((("q", c.i_aw_qT), ("k", c.i_aw_kT),
                                      ("v", c.i_aw_vT), ("o", c.i_aw_oT))):
        for k in range(KT):
            lw[name] = c.dq_wl.issue(lambda e, wi=wi, k=k, src=src: e.dma_start(
                out=c.b16(o_w + wi * 16384 + k * 2048, 128, 1024), in_=src[k]))

    tk = {"sp_free": [None, None], "pp_free": [None, None],
          "pa_free": [None, None], "pn_free": [None, None],
          "ps_free": [None, None], "pr_free": [None, None],
          "rec_free": [None, None], "tro_free": [None, None],
          "x2_free": [None, None], "nx_free": [None, None],
          "atm_free": [None, None],
          "ktm_done": None, "v_done": None, "q_done": None, "kfm_done": None,
          "Sbf": [None] * 16, "Sadd": None, "out_ev": [None] * 8}
    first_state = [True]

    p.do("vector", lambda e: e.memset(
        c.A16[:, o_vp // 2: o_vp // 2 + 8 * 1040].rearrange(
            "p (h z) -> p h z", z=65)[:, :, 64:65], 1.0), inc=c.s_dv[1])

    def proj_phase(qi, prefix, wy, ev_y_q):
        PHX = os.environ.get("KDBG_PH", "all")
        NU = int(os.environ.get("KDBG_NU", "99"))
        NOEV = os.environ.get("KDBG_NOEV", "0") == "1"
        if not prefix:
            for which in ("q", "k"):
                wi = 0 if which == "q" else 1
                boff = c.o_awbq if which == "q" else c.o_awbk
                dsto = o_q if which == "q" else o_kfm
                last = None
                for d in range(8):
                    for c2 in range(2):
                        if d * 2 + c2 >= NU:
                            continue
                        ps = (d * 2 + c2) % 2
                        wts = [(c.s_ac[0], ev_y_q[4 * c2 + 3]), lw[which],
                               tk["pp_free"][ps],
                               tk["pn_free"][0], tk["pn_free"][1],
                               tk["pr_free"][0], tk["pr_free"][1]]
                        pv = None
                        for k in range(KT):
                            pv = p.do("tensor", lambda e, d=d, c2=c2, k=k, ps=ps, wi=wi: e.matmul(
                                c.P32[:, A_PP[ps]:A_PP[ps] + 512],
                                wmat(wi, k, d * 128, 128), yq(k, c2 * 512, 512),
                                start=(k == 0), stop=(k == KT - 1)),
                                waits=wts if k == 0 else [],
                                inc=c.s_pe[1] if k == KT - 1 else None)
                        if NOEV:
                            tk["pp_free"][ps] = (c.s_pe[1], pv)
                            last = None
                            continue
                        EV = os.environ.get("KDBG_EV", "expln")
                        if EV == "cp":
                            av = p.do("scalar", lambda e, ps=ps: e.copy(
                                c.b32(o_sp + ps * 2048, 128, 512),
                                c.P32[:, A_PP[ps]:A_PP[ps] + 512]),
                                waits=[(c.s_pe[1], pv), tk["sp_free"][ps]],
                                inc=c.s_ac[1])
                            tk["sp_free"][ps] = (c.s_ac[1], av)
                            tk["pp_free"][ps] = (c.s_ac[1], av)
                            last = None
                            continue
                        if EV == "exp":
                            av = p.do("scalar", lambda e, ps=ps, d=d, boff=boff: e.activation(
                                c.b32(o_sp + ps * 2048, 128, 512),
                                c.P32[:, A_PP[ps]:A_PP[ps] + 512], AF.Exp,
                                bias=c.A32[:, boff // 4 + d: boff // 4 + d + 1]),
                                waits=[(c.s_pe[1], pv), tk["sp_free"][ps]],
                                inc=c.s_ac[1])
                            tk["sp_free"][ps] = (c.s_ac[1], av)
                            tk["pp_free"][ps] = (c.s_ac[1], av)
                            last = None
                            continue
                        p.do("scalar", lambda e, ps=ps, d=d, boff=boff: e.activation(
                            c.b32(o_sp + ps * 2048, 128, 512),
                            c.P32[:, A_PP[ps]:A_PP[ps] + 512], AF.Exp,
                            bias=c.A32[:, boff // 4 + d: boff // 4 + d + 1]),
                            waits=[(c.s_pe[1], pv), tk["sp_free"][ps]])
                        av = p.do("scalar", lambda e, ps=ps, d=d, c2=c2, dsto=dsto: e.activation(
                            c.b16(dsto + d * 2048 + c2 * 1024, 128, 512),
                            c.b32(o_sp + ps * 2048, 128, 512), AF.Ln, bias=1.0),
                            inc=c.s_ac[1])
                        tk["sp_free"][ps] = (c.s_ac[1], av)
                        tk["pp_free"][ps] = (c.s_ac[1], av)
                        last = av
                if last is not None:
                    tk["q_done" if which == "q" else "kfm_done"] = (c.s_ac[1], last)
        if PHX == "projq":
            return
        uslot = [0]
        for tt in range(8):
            for n in range(2):
                ps = uslot[0] % 2
                uslot[0] += 1
                wts0 = [(c.s_ac[0], ev_y_q[tt]), lw["k"], tk["pp_free"][ps],
                        tk["pn_free"][0], tk["pn_free"][1],
                        tk["pr_free"][0], tk["pr_free"][1]]
                for k in range(KT):
                    p.do("tensor", lambda e, tt=tt, n=n, k=k, ps=ps: e.matmul(
                        c.P32[:, A_PP[ps]:A_PP[ps] + 512],
                        yq(k, tt * 128, 128), wmat(1, k, n * 512, 512),
                        start=(k == 0), stop=False),
                        waits=wts0 if k == 0 else [])
                pv = p.do("tensor", lambda e, n=n, ps=ps: e.matmul(
                    c.P32[:, A_PP[ps]:A_PP[ps] + 512], c.ones,
                    c.row16(0, n * 512, 512), start=False, stop=True),
                    inc=c.s_pe[1])
                p.do("scalar", lambda e, ps=ps: e.activation(
                    c.b32(o_sp + ps * 2048, 128, 512),
                    c.P32[:, A_PP[ps]:A_PP[ps] + 512], AF.Exp),
                    waits=[(c.s_pe[1], pv), tk["sp_free"][ps]])
                av = p.do("scalar", lambda e, tt=tt, n=n, ps=ps: e.activation(
                    c.b16(o_ktm + tt * 2048 + n * 1024, 128, 512),
                    c.b32(o_sp + ps * 2048, 128, 512), AF.Ln, bias=1.0),
                    inc=c.s_ac[1])
                tk["sp_free"][ps] = (c.s_ac[1], av)
                if prefix:
                    dvm = p.do("vector", lambda e, tt=tt, n=n: e.tensor_scalar(
                        c.b16(o_ktm + tt * 2048 + n * 1024, 128, 512),
                        c.b16(o_ktm + tt * 2048 + n * 1024, 128, 512),
                        c.pmask, None, ALU.mult),
                        waits=[(c.s_ac[1], av)], inc=c.s_dv[1])
                    tk["ktm_done"] = (c.s_dv[1], dvm)
                else:
                    tk["ktm_done"] = (c.s_ac[1], av)
                tk["pp_free"][ps] = tk["ktm_done"]
            if PHX == "projk":
                continue
            for n in range(2):
                ps = uslot[0] % 2
                uslot[0] += 1
                wtsv = [(c.s_ac[0], ev_y_q[tt]), lw["v"], tk["pp_free"][ps]]
                for k in range(KT):
                    p.do("tensor", lambda e, tt=tt, n=n, k=k, ps=ps: e.matmul(
                        c.P32[:, A_PP[ps]:A_PP[ps] + 512],
                        yq(k, tt * 128, 128), wmat(2, k, n * 512, 512),
                        start=(k == 0), stop=False),
                        waits=wtsv if k == 0 else [])
                pv = p.do("tensor", lambda e, n=n, ps=ps: e.matmul(
                    c.P32[:, A_PP[ps]:A_PP[ps] + 512], c.ones,
                    c.row16(1, n * 512, 512), start=False, stop=True),
                    inc=c.s_pe[1])
                vpr = c.A16[:, (o_vp // 2) + tt * 1040: (o_vp // 2) + (tt + 1) * 1040].rearrange(
                    "p (h z) -> p h z", z=65)
                dv = p.do("vector", lambda e, n=n, ps=ps, vpr=vpr: e.tensor_copy(
                    vpr[:, n * 8:(n + 1) * 8, 0:64],
                    c.P32[:, A_PP[ps]:A_PP[ps] + 512].rearrange(
                        "p (h z) -> p h z", z=64)),
                    waits=[(c.s_pe[1], pv)], inc=c.s_dv[1])
                tk["pp_free"][ps] = (c.s_dv[1], dv)
                tk["v_done"] = (c.s_dv[1], dv)

    def state_mm(cc, hh, waits, defer=True):
        g = hh // 2
        so = A_PS[g % 2] + 65 * (hh % 2)
        vp = c.A16[:, (o_vp // 2) + cc * 1040 + 65 * hh:
                   (o_vp // 2) + cc * 1040 + 65 * hh + 65]
        kt = c.A16[:, (o_ktm // 2) + cc * 1024 + 64 * hh:
                   (o_ktm // 2) + cc * 1024 + 64 * hh + 64]
        p.do("tensor", lambda e, kt=kt, vp=vp, so=so: e.matmul(
            c.P32[0:64, so:so + 65], kt, vp, start=True, stop=True,
            tile_position=(0, 0)), waits=waits)
        pst = p.do("tensor", lambda e, kt=kt, vp=vp, so=so: e.matmul(
            c.P32[64:128, so:so + 65], kt, vp, start=True,
            stop=True, tile_position=(0, 64)), inc=c.s_pe[2])
        if hh % 2 == 0:
            return None
        h0 = hh - 1
        so0 = A_PS[g % 2]
        if first_state[0]:
            dv = p.do("vector", lambda e, h0=h0, so0=so0: e.tensor_copy(
                c.b32(o_S32 + 65 * h0 * 4, 128, 130),
                c.P32[:, so0:so0 + 130]),
                waits=[(c.s_pe[2], pst)], inc=c.s_dv[1])
        else:
            dv = p.do("vector", lambda e, h0=h0, so0=so0: e.tensor_add(
                c.b32(o_S32 + 65 * h0 * 4, 128, 130),
                c.b32(o_S32 + 65 * h0 * 4, 128, 130),
                c.P32[:, so0:so0 + 130]),
                waits=[(c.s_pe[2], pst)], inc=c.s_dv[1])
        return dv

    def scan_chunk(cc):
        def emit_A(h):
            sl = h % 2
            base = 64 * (h % 2)
            dtile = h // 2
            kf = c.A16[base:base + 64, (o_kfm // 2) + dtile * 1024 + cc * 128:
                       (o_kfm // 2) + dtile * 1024 + cc * 128 + 128]
            qf = c.A16[base:base + 64, (o_q // 2) + dtile * 1024 + cc * 128:
                       (o_q // 2) + dtile * 1024 + cc * 128 + 128]
            pv = p.do("tensor", lambda e, kf=kf, qf=qf, sl=sl: e.matmul(
                c.P32[:, A_PA[sl]:A_PA[sl] + 128], kf, qf, start=True, stop=True),
                waits=[tk["q_done"], tk["kfm_done"], tk["pa_free"][sl]],
                inc=c.s_pe[2])
            atm = c.b16(o_atm + sl * 256, 128, 128)
            dv = p.do("vector", lambda e, sl=sl, atm=atm: e.tensor_tensor(
                atm, c.P32[:, A_PA[sl]:A_PA[sl] + 128], c.trimask, ALU.mult),
                waits=[(c.s_pe[2], pv), tk["atm_free"][sl]], inc=c.s_dv[1])
            tk["pa_free"][sl] = (c.s_dv[1], dv)
            return (c.s_dv[1], dv), atm

        pend = None
        last_ev = None
        for h in range(17):
            cur = emit_A(h) if h < 16 else None
            if pend is not None:
                hh, maskev, atm = pend
                sl = hh % 2
                base = 64 * (hh % 2)
                dtile = hh // 2
                vp = c.A16[:, (o_vp // 2) + cc * 1040 + 65 * hh:
                           (o_vp // 2) + cc * 1040 + 65 * hh + 65]
                qf = c.A16[base:base + 64, (o_q // 2) + dtile * 1024 + cc * 128:
                           (o_q // 2) + dtile * 1024 + cc * 128 + 128]
                sb = c.A16[base:base + 64, (o_S16 // 2) + 65 * hh:
                           (o_S16 // 2) + 65 * hh + 65]
                p.do("tensor", lambda e, atm=atm, vp=vp, sl=sl: e.matmul(
                    c.P32[:, A_PN[sl]:A_PN[sl] + 65], atm, vp,
                    start=True, stop=False),
                    waits=[maskev, tk["v_done"], tk["pn_free"][sl],
                           tk["Sbf"][hh]])
                pnum = p.do("tensor", lambda e, qf=qf, sb=sb, sl=sl: e.matmul(
                    c.P32[:, A_PN[sl]:A_PN[sl] + 65], qf, sb,
                    start=False, stop=True), inc=c.s_pe[2])
                w = ([tk["ktm_done"], tk["ps_free"][(hh // 2) % 2]]
                     if hh % 2 == 0 else [])
                dvst = state_mm(cc, hh, w)
                rec = c.b32(o_rec + sl * 64, 128, 1)
                dvr = p.do("vector", lambda e, rec=rec, sl=sl: e.reciprocal(
                    rec, c.P32[:, A_PN[sl] + 64:A_PN[sl] + 65]),
                    waits=[(c.s_pe[2], pnum), tk["rec_free"][sl]],
                    inc=c.s_dv[1])
                aev = p.do("scalar", lambda e, hh=hh, rec=rec, sl=sl: e.activation(
                    c.b16(o_out + cc * 2048 + 64 * hh * 2, 128, 64),
                    c.P32[:, A_PN[sl]:A_PN[sl] + 64], AF.Copy, scale=rec),
                    waits=[(c.s_dv[1], dvr)], inc=c.s_ac[2])
                tk["rec_free"][sl] = (c.s_ac[2], aev)
                tk["pn_free"][sl] = (c.s_ac[2], aev)
                tk["atm_free"][sl] = (c.s_pe[2], pnum)
                last_ev = aev
                if hh % 2 == 1:
                    h0 = hh - 1
                    dsb = p.do("vector", lambda e, h0=h0: e.tensor_copy(
                        c.b16(o_S16 + 65 * h0 * 2, 128, 130),
                        c.b32(o_S32 + 65 * h0 * 4, 128, 130)), inc=c.s_dv[1])
                    tk["Sbf"][h0] = (c.s_dv[1], dsb)
                    tk["Sbf"][h0 + 1] = (c.s_dv[1], dsb)
                    tk["ps_free"][(hh // 2) % 2] = (c.s_dv[1], dsb)
            pend = (h, cur[0], cur[1]) if h < 16 else None
        tk["out_ev"][cc] = (c.s_ac[2], last_ev)

    def wo_quarter(qi):
        for cc in range(8):
            gt = qi * 8 + cc
            sl = cc % 2
            evx = c.dq_x.issue(lambda e, gt=gt, sl=sl: e.dma_start(
                out=c.b32(o_x2 + sl * 4096, 128, 1024),
                in_=xin[gt * 128:(gt + 1) * 128, :]),
                waits=[tk["x2_free"][sl]])
            for j in range(KT):
                ts = A_TRO[j % 2]
                wts = [tk["out_ev"][cc], tk["tro_free"][j % 2],
                       tk["pa_free"][0], tk["pa_free"][1]]
                pv = p.do("tensor", lambda e, j=j, ts=ts, cc=cc: e.transpose(
                    c.P16[:, ts:ts + 128],
                    c.b16(o_out + cc * 2048 + j * 256, 128, 128), c.ident),
                    waits=wts, inc=c.s_pe[3])
                av = p.do("scalar", lambda e, j=j, ts=ts, sl=sl: e.copy(
                    c.b16(o_ofm + sl * 2048 + j * 256, 128, 128),
                    c.P16[:, ts:ts + 128]),
                    waits=[(c.s_pe[3], pv)], inc=c.s_ac[2])
                tk["tro_free"][j % 2] = (c.s_ac[2], av)
            aofm = tk["tro_free"][(KT - 1) % 2]
            dvn = None
            for n in range(2):
                wts = [aofm, tk["pr_free"][n], tk["pn_free"][n],
                       tk["ps_free"][0], tk["ps_free"][1]]
                for j in range(KT):
                    p.do("tensor", lambda e, j=j, n=n, sl=sl: e.matmul(
                        c.P32[:, A_PR[n]:A_PR[n] + 512],
                        c.b16(o_ofm + sl * 2048 + j * 256, 128, 128),
                        wmat(3, j, n * 512, 512), start=(j == 0), stop=False),
                        waits=wts if j == 0 else [])
                pv = p.do("tensor", lambda e, n=n: e.matmul(
                    c.P32[:, A_PR[n]:A_PR[n] + 512], c.ones,
                    c.row16(2, n * 512, 512), start=False, stop=True),
                    inc=c.s_pe[3])
                dvn = p.do("vector", lambda e, n=n, sl=sl: e.tensor_add(
                    c.b32(o_nx + sl * 4096 + n * 2048, 128, 512),
                    c.b32(o_x2 + sl * 4096 + n * 2048, 128, 512),
                    c.P32[:, A_PR[n]:A_PR[n] + 512]),
                    waits=[(c.s_pe[3], pv), evx, tk["nx_free"][sl]],
                    inc=c.s_dv[2])
                tk["pr_free"][n] = (c.s_dv[2], dvn)
            stv = c.dq_st.issue(lambda e, gt=gt, sl=sl: e.dma_start(
                out=xout[gt * 128:(gt + 1) * 128, :],
                in_=c.b32(o_nx + sl * 4096, 128, 1024)),
                waits=[(c.s_dv[2], dvn)])
            tk["nx_free"][sl] = stv
            tk["x2_free"][sl] = (c.s_dv[2], dvn)

    PH = os.environ.get("KDBG_PH", "all")

    def quarter(qi, prefix):
        xsrc = c.x_pre if prefix else xin
        ev_y = {}
        if os.environ.get("KDBG_LN", "1") == "0":
            mv = p.do("vector", lambda e: e.memset(
                c.b16(o_y, 128, 8192), 0.001), inc=c.s_dv[0])
            acv = p.do("scalar", lambda e: e.copy(
                c.b16(o_y, 128, 2), c.b16(o_y, 128, 2)), inc=c.s_ac[0])
            for tt in range(8):
                ev_y[tt] = acv
            wy = (c.s_ac[0], acv)
            if PH == "ln":
                return
            proj_phase(qi, prefix, wy, ev_y)
            return
        for tt in range(8):
            gt = qi * 8 + tt
            evx = c.dq_x.issue(lambda e, gt=gt, tt=tt, xsrc=xsrc: e.dma_start(
                out=c.b32(o_x2 + (tt % 2) * 4096, 128, 1024),
                in_=xsrc[gt * 128:(gt + 1) * 128, :]),
                waits=[tk["x2_free"][tt % 2]])
            ydst = lambda j, tt=tt: yq(j, tt * 128, 128)
            ev_y[tt] = _ln_tile(c, c.b32(o_x2 + (tt % 2) * 4096, 128, 1024),
                                ydst, tt, [evx], A_TRLN, lnb)
            tk["x2_free"][tt % 2] = (c.s_ac[0], ev_y[tt])
        wy = (c.s_ac[0], ev_y[7])
        if PH == "ln":
            return
        proj_phase(qi, prefix, wy, ev_y)
        if PH.startswith("proj"):
            return
        if prefix:
            for cc in range(8):
                for hh in range(16):
                    g = hh // 2
                    w = ([tk["ktm_done"], tk["v_done"], tk["ps_free"][g % 2]]
                         if hh % 2 == 0 else [])
                    dv = state_mm(cc, hh, w)
                    if dv is not None:
                        tk["ps_free"][g % 2] = (c.s_dv[1], dv)
                        tk["Sadd"] = (c.s_dv[1], dv)
                if first_state[0]:
                    first_state[0] = False
        else:
            for cc in range(8):
                scan_chunk(cc)
            if PH == "scan":
                return
            wo_quarter(qi)

    NPRE = int(os.environ.get("KDBG_PRE", "4"))
    NOWN = int(os.environ.get("KDBG_OWN", "4"))
    for qi in range(NPRE):
        quarter(qi, prefix=True)
    if NPRE > 0 and PH == "all":
        sbv = p.do("vector", lambda e: e.tensor_copy(
            c.b16(o_S16, 128, 1040), c.b32(o_S32, 128, 1040)),
            waits=[tk["Sadd"]], inc=c.s_dv[1])
    else:
        p.do("vector", lambda e: e.memset(c.b32(o_S32, 128, 1040), 0.0))
        sbv = p.do("vector", lambda e: e.memset(c.b16(o_S16, 128, 1040), 0.0),
                   inc=c.s_dv[1])
        first_state[0] = False
    for h in range(16):
        tk["Sbf"][h] = (c.s_dv[1], sbv)
    for qi in range(NOWN):
        quarter(qi, prefix=False)


def _attn_b(c, xin, xout):
    p = c.p
    B = c.POOL
    o_y = B
    o_wv = o_y + 65536
    o_x2 = o_wv + 16384
    o_nx = o_x2 + 8192
    o_ln = o_nx + 8192
    assert o_ln + LNB_BYTES <= ARENA_BYTES

    lnb = _new_lnb(o_ln)
    yfm = lambda k, c0, n: c.b16(o_y + k * 8192 + c0 * 2, 128, n)
    PV = [0, 1024]
    TRLN = [6400, 7168]

    lwv = None
    for k in range(KT):
        lwv = c.dq_wl.issue(lambda e, k=k: e.dma_start(
            out=c.b16(o_wv + k * 2048, 128, 1024), in_=c.i_bw_vT[k]))

    tk = {"x2_free": [None, None], "nx_free": [None, None],
          "pv_free": [None, None]}
    ev_y = {}
    for tt in range(TILES):
        evx = c.dq_x.issue(lambda e, tt=tt: e.dma_start(
            out=c.b32(o_x2 + (tt % 2) * 4096, 128, 1024),
            in_=xin[tt * 128:(tt + 1) * 128, :]),
            waits=[tk["x2_free"][tt % 2]])
        ydst = lambda j, tt=tt: yfm(j, tt * 128, 128)
        ev_y[tt] = _ln_tile(c, c.b32(o_x2 + (tt % 2) * 4096, 128, 1024),
                            ydst, tt, [evx], TRLN, lnb)
        tk["x2_free"][tt % 2] = (c.s_ac[0], ev_y[tt])

    for tt in range(TILES):
        sl = tt % 2
        evx2 = c.dq_x.issue(lambda e, tt=tt, sl=sl: e.dma_start(
            out=c.b32(o_x2 + sl * 4096, 128, 1024),
            in_=xin[tt * 128:(tt + 1) * 128, :]),
            waits=[tk["x2_free"][sl]])
        for n in range(2):
            wts = [(c.s_ac[0], ev_y[tt]), lwv, tk["pv_free"][sl]]
            for k in range(KT):
                p.do("tensor", lambda e, tt=tt, n=n, k=k, sl=sl: e.matmul(
                    c.P32[:, PV[sl] + n * 512:PV[sl] + n * 512 + 512],
                    yfm(k, tt * 128, 128),
                    c.b16(o_wv + k * 2048 + n * 512 * 2, 128, 512),
                    start=(k == 0), stop=False),
                    waits=wts if (k == 0 and n == 0) else [])
            pv = p.do("tensor", lambda e, n=n, sl=sl: e.matmul(
                c.P32[:, PV[sl] + n * 512:PV[sl] + n * 512 + 512],
                c.ones, c.row16(3, n * 512, 512), start=False, stop=True),
                inc=c.s_pe[3] if n == 1 else None)
        dv = p.do("vector", lambda e, sl=sl: e.tensor_add(
            c.b32(o_nx + sl * 4096, 128, 1024),
            c.b32(o_x2 + sl * 4096, 128, 1024),
            c.P32[:, PV[sl]:PV[sl] + 1024]),
            waits=[(c.s_pe[3], pv), evx2, tk["nx_free"][sl]],
            inc=c.s_dv[2])
        tk["pv_free"][sl] = (c.s_dv[2], dv)
        stv = c.dq_st.issue(lambda e, tt=tt, sl=sl: e.dma_start(
            out=xout[tt * 128:(tt + 1) * 128, :],
            in_=c.b32(o_nx + sl * 4096, 128, 1024)),
            waits=[(c.s_dv[2], dv)])
        tk["nx_free"][sl] = stv
        tk["x2_free"][sl] = (c.s_dv[2], dv)


def _stage(inputs):
    f = np.ascontiguousarray
    s = {}
    s["trimask"] = f(np.triu(np.ones((128, 128), dtype=np.float32)))
    s["identb"] = np.eye(128, dtype=ml_dtypes.bfloat16)
    for nm, w in (("aw_qT", inputs["L0_attn_a_Wq"]), ("aw_kT", inputs["L0_attn_a_Wk"]),
                  ("aw_vT", inputs["L0_attn_a_Wv"]), ("aw_oT", inputs["L0_attn_a_Wo"]),
                  ("bw_vT", inputs["L0_attn_b_Wv"])):
        s[nm] = f(np.asarray(w).T.reshape(KT, 128, D))
    s["aw_bq"] = f(np.asarray(inputs["L0_attn_a_bq"]).reshape(KT, 128).T)
    s["aw_bk"] = f(np.asarray(inputs["L0_attn_a_bk"]).reshape(KT, 128).T)
    s["rows"] = f(np.stack([np.asarray(inputs["L0_attn_a_bk"]),
                            np.asarray(inputs["L0_attn_a_bv"]),
                            np.asarray(inputs["L0_attn_a_bo"]),
                            np.asarray(inputs["L0_attn_b_bv"]),
                            np.asarray(inputs["L0_ffn_s_fc2_b"])]).astype(np.float32).reshape(1, 5 * D))
    s["gateS"] = f(np.asarray(inputs["L0_ffn_gate_w"]).T.reshape(KT, 128, E))
    F1 = np.asarray(inputs["L0_ffn_fc1_w"]).transpose(0, 2, 1)
    s["fc1S"] = f(F1.reshape(E, KT, 128, 2, 8, 128).transpose(0, 4, 2, 1, 3, 5).reshape(E, 8, 128, 2048))
    B1 = np.asarray(inputs["L0_ffn_fc1_b"]).reshape(E, 2, 8, 128)
    s["fc1bS"] = f(B1.transpose(3, 0, 2, 1).reshape(128, E * 16))
    s["multS"] = f(np.asarray(inputs["L0_ffn_geglu_mult"]).reshape(E, 8, 128).transpose(2, 0, 1).reshape(128, E * 8))
    s["fc2S"] = f(np.asarray(inputs["L0_ffn_fc2_w"]).transpose(0, 2, 1).reshape(E, KT, 128, 1024))
    s["fc2bS"] = f(np.asarray(inputs["L0_ffn_fc2_b"]).astype(np.float32))
    S1 = np.asarray(inputs["L0_ffn_s_fc1_w"]).T
    s["sfc1S"] = f(S1.reshape(KT, 128, 2, 16, 128).transpose(3, 1, 0, 2, 4).reshape(16, 128, 2048))
    SB = np.asarray(inputs["L0_ffn_s_fc1_b"]).reshape(2, 16, 128)
    s["sfc1bS"] = f(SB.transpose(2, 1, 0).reshape(128, 32))
    s["smultS"] = f(np.asarray(inputs["L0_ffn_s_geglu_mult"]).reshape(16, 128).T)
    s["sfc2S"] = f(np.asarray(inputs["L0_ffn_s_fc2_w"]).T.reshape(16, 128, 1024))
    return s


def kernel(**inputs):
    x = np.asarray(inputs["L0_attn_a_x"], dtype=np.float32)
    shared = _stage(inputs)
    nc = build(NSUB)
    in_maps = []
    for core in range(8):
        b, h = core // 2, core % 2
        m = dict(shared)
        m["x_own"] = np.ascontiguousarray(x[b, h * NT:(h + 1) * NT, :])
        m["x_pre"] = np.ascontiguousarray(x[b, 0:NT, :])
        m["pmask"] = np.full((128, 1), float(h), dtype=np.float32)
        in_maps.append(m)
    res = run_bass_kernel_spmd(nc, in_maps, core_ids=list(range(8)))
    out = np.empty((4, 8192, D), dtype=np.float32)
    for core in range(8):
        b, h = core // 2, core % 2
        out[b, h * NT:(h + 1) * NT, :] = res.results[core]["y"]
    return out


# version 11
# speedup vs baseline: 1.0850x; 1.0850x over previous
import os
import numpy as np
import ml_dtypes
import concourse.bass as bass
import concourse.mybir as mybir
from concourse.bass_utils import run_bass_kernel_spmd

F32 = mybir.dt.float32
BF16 = mybir.dt.bfloat16
U8 = mybir.dt.uint8
AF = mybir.ActivationFunctionType
ALU = mybir.AluOpType
AX = mybir.AxisListType

D = 1024
NT = 4096
TILES = 32
KT = 8
NSUB = 4
E = 15
LNEPS = 1e-5
ARENA_BYTES = 212608

ENGS = ("sync", "tensor", "vector", "scalar", "gpsimd")


class Sem:
    __slots__ = ("name", "n", "h", "dma")

    def __init__(self, name, dma=False):
        self.name, self.n, self.h, self.dma = name, 0, None, dma


class Prog:
    def __init__(self, nc):
        self.nc = nc
        self.ops = {e: [] for e in ENGS}
        self.sems = []
        self.hiwater = {e: {} for e in ENGS}
        self.auto = {}

    def sem(self, name, dma=False):
        s = Sem(name, dma)
        self.sems.append(s)
        return s

    def do(self, eng, fn, waits=(), inc=None, inc_amt=1):
        auto = self.auto.get(eng)
        if auto is not None and fn is not None:
            waits = list(waits) + [(auto, auto.n)]
            if inc is None:
                inc = auto
                inc_amt = 1
            else:
                assert inc is auto, (eng, inc.name)
        ws = []
        hw = self.hiwater[eng]
        for w in waits:
            if w is None:
                continue
            s, v = w
            assert v <= s.n, f"wait {s.name}>={v} but only {s.n} emitted"
            if hw.get(s.name, -1) >= v:
                continue
            hw[s.name] = v
            ws.append((s, v))
        after = None
        if inc is not None:
            inc.n += inc_amt
            after = inc.n
        self.ops[eng].append((ws, fn, inc, inc_amt))
        return after

    def barrier(self, b, engines=ENGS):
        for e in engines:
            auto = self.auto.get(e)
            if auto is not None:
                self.do(e, None, waits=[(auto, auto.n)])
            self.ops[e].append(([], (lambda eng: eng.nop()), b, 1))
            b.n += 1
        tgt = b.n
        for e in engines:
            self.do(e, None, waits=[(b, tgt)])

    def realize(self):
        from contextlib import ExitStack
        nc = self.nc
        with ExitStack() as st:
            for s in self.sems:
                s.h = st.enter_context(nc.semaphore(s.name))
            blk = st.enter_context(nc.Block())
            for e in ENGS:
                ops = self.ops[e]

                def body(eng, ops=ops):
                    for ws, fn, inc, amt in ops:
                        for s, v in ws:
                            eng.wait_ge(s.h, v)
                        if fn is not None:
                            inst = fn(eng)
                            if inc is not None:
                                inst.then_inc(inc.h, amt)
                getattr(blk, e)(body)


class DmaQ:

    def __init__(self, p, name, eng, k=4):
        self.p, self.eng, self.k = p, eng, k
        self.sems = [p.sem(f"{name}{i}") for i in range(k)]
        self.counts = [0] * k
        self.i = 0

    def issue(self, fn, waits=()):
        j = self.i % self.k
        self.i += 1
        s = self.sems[j]
        w = list(waits)
        if self.counts[j] > 0:
            w.append((s, 16 * self.counts[j]))
        self.counts[j] += 1
        self.p.do(self.eng, fn, waits=w, inc=s, inc_amt=16)
        return (s, 16 * self.counts[j])

    def drain_waits(self):
        return [(s, 16 * n) for s, n in zip(self.sems, self.counts) if n]


def _align(x, a=128):
    return (x + a - 1) & ~(a - 1)


class Ctx:
    pass


def build(nsub=NSUB):
    nc = bass.Bass("TRN2", target_bir_lowering=False,
                   detect_race_conditions=False)
    c = Ctx()
    c.nc = nc
    p = Prog(nc)
    c.p = p

    dt = nc.dram_tensor
    c.x_own = dt("x_own", [NT, D], F32, kind="ExternalInput").ap()
    c.x_pre = dt("x_pre", [NT, D], F32, kind="ExternalInput").ap()
    c.i_pmask = dt("pmask", [128, 1], F32, kind="ExternalInput").ap()
    c.i_trimask = dt("trimask", [128, 128], F32, kind="ExternalInput").ap()
    c.i_ident = dt("identb", [128, 128], BF16, kind="ExternalInput").ap()
    for n in ("aw_qT", "aw_kT", "aw_vT", "aw_oT", "bw_vT"):
        setattr(c, "i_" + n, dt(n, [KT, 128, D], F32, kind="ExternalInput").ap())
    c.i_aw_bq = dt("aw_bq", [128, KT], F32, kind="ExternalInput").ap()
    c.i_aw_bk = dt("aw_bk", [128, KT], F32, kind="ExternalInput").ap()
    c.i_rows = dt("rows", [1, 5 * D], F32, kind="ExternalInput").ap()
    c.i_gate = dt("gateS", [KT, 128, E], F32, kind="ExternalInput").ap()
    c.i_fc1S = dt("fc1S", [E, 8, 128, 2048], F32, kind="ExternalInput").ap()
    c.i_fc1b = dt("fc1bS", [128, E * 16], F32, kind="ExternalInput").ap()
    c.i_mult = dt("multS", [128, E * 8], F32, kind="ExternalInput").ap()
    c.i_fc2S = dt("fc2S", [E, 8, 128, 1024], F32, kind="ExternalInput").ap()
    c.i_fc2b = dt("fc2bS", [E, 1024], F32, kind="ExternalInput").ap()
    c.i_sfc1S = dt("sfc1S", [16, 128, 2048], F32, kind="ExternalInput").ap()
    c.i_sfc1b = dt("sfc1bS", [128, 32], F32, kind="ExternalInput").ap()
    c.i_smult = dt("smultS", [128, 16], F32, kind="ExternalInput").ap()
    c.i_sfc2S = dt("sfc2S", [16, 128, 1024], F32, kind="ExternalInput").ap()
    c.y_out = dt("y", [NT, D], F32, kind="ExternalOutput").ap()
    c.xs0 = dt("xs0", [NT, D], F32, kind="Internal").ap()
    c.xs1 = dt("xs1", [NT, D], F32, kind="Internal").ap()

    c.arena = nc.alloc_sbuf_tensor("arena", [128, ARENA_BYTES], U8)
    c.A32 = c.arena.bitcast(F32).ap()
    c.A16 = c.arena.bitcast(BF16).ap()

    def b32(off, parts, n):
        assert off % 4 == 0 and off + 4 * n <= ARENA_BYTES, (off, n)
        return c.A32[0:parts, off // 4: off // 4 + n]

    def b16(off, parts, n):
        assert off % 2 == 0 and off + 2 * n <= ARENA_BYTES, (off, n)
        return c.A16[0:parts, off // 2: off // 2 + n]
    c.b32, c.b16 = b32, b16

    off = 0

    def take(nbytes):
        nonlocal off
        o = off
        off = _align(off + nbytes)
        return o
    c.o_ident = take(128 * 2)
    c.o_trimask = take(128 * 4)
    c.o_pmask = take(64)
    c.o_rows = take(5 * D * 2)
    c.o_ones = take(128 * 2)
    c.o_gate = take(KT * E * 2)
    c.o_fc1b = take(E * 16 * 4)
    c.o_mult = take(E * 8 * 4)
    c.o_sfc1b = take(32 * 4)
    c.o_smult = take(16 * 4)
    c.o_fc2b = take(1024 * 2)
    c.o_awbq = take(KT * 4)
    c.o_awbk = take(KT * 4)
    c.o_eps = take(64)
    c.POOL = _align(off)
    assert c.POOL < 16 * 1024, c.POOL

    c.psum_h = nc.alloc_psum_tensor("psum", [128, 4096], F32)
    c.P32 = c.psum_h.ap()
    c.P16 = c.psum_h.bitcast(BF16).ap()

    c.bar = p.sem("bar")
    c.dq_x = DmaQ(p, "qx", "sync", 4)
    c.dq_st = DmaQ(p, "qs", "sync", 4)
    c.dq_wl = DmaQ(p, "qw", "gpsimd", 4)
    c.s_pe = [p.sem(f"s_pe{i}") for i in range(4)]
    _dv = p.sem("s_dv")
    _ac = p.sem("s_ac")
    c.s_dv = [_dv] * 4
    c.s_ac = [_ac] * 4
    p.auto["vector"] = _dv
    p.auto["scalar"] = _ac

    c.ident = b16(c.o_ident, 128, 128)
    c.trimask = b32(c.o_trimask, 128, 128)
    c.pmask = b32(c.o_pmask, 128, 1)
    c.ones = b16(c.o_ones, 1, 128)

    def row16(i, j0, n):
        base = c.o_rows // 2 + i * D
        return c.A16[0:1, base + j0: base + j0 + n]
    c.row16 = row16

    _preload(c)
    p.barrier(c.bar)

    if nsub == 1:
        _attn_a(c, c.x_own, c.y_out)
        p.barrier(c.bar)
    elif nsub == 2:
        _attn_a(c, c.x_own, c.xs0)
        p.barrier(c.bar)
        _ffn(c, c.xs0, c.y_out, fuse_attnb=False)
        p.barrier(c.bar)
    elif nsub == 3:
        _attn_a(c, c.x_own, c.xs0)
        p.barrier(c.bar)
        _ffn(c, c.xs0, c.y_out, fuse_attnb=True)
        p.barrier(c.bar)
    else:
        _attn_a(c, c.x_own, c.xs0)
        p.barrier(c.bar)
        _ffn(c, c.xs0, c.xs1, fuse_attnb=True)
        p.barrier(c.bar)
        _ffn(c, c.xs1, c.y_out, fuse_attnb=False)
        p.barrier(c.bar)

    p.do("sync", None, waits=c.dq_st.drain_waits())
    p.realize()
    return nc


def _preload(c):
    p = c.p

    def g(fn):
        return c.dq_wl.issue(fn)

    def s(fn):
        return c.dq_x.issue(fn)
    s(lambda e: e.dma_start(out=c.ident, in_=c.i_ident))
    s(lambda e: e.dma_start(out=c.trimask, in_=c.i_trimask))
    s(lambda e: e.dma_start(out=c.pmask, in_=c.i_pmask))
    s(lambda e: e.dma_start(out=c.b32(c.o_fc1b, 128, E * 16), in_=c.i_fc1b))
    s(lambda e: e.dma_start(out=c.b32(c.o_mult, 128, E * 8), in_=c.i_mult))
    s(lambda e: e.dma_start(out=c.b32(c.o_sfc1b, 128, 32), in_=c.i_sfc1b))
    s(lambda e: e.dma_start(out=c.b32(c.o_smult, 128, 16), in_=c.i_smult))
    s(lambda e: e.dma_start(out=c.b32(c.o_awbq, 128, KT), in_=c.i_aw_bq))
    s(lambda e: e.dma_start(out=c.b32(c.o_awbk, 128, KT), in_=c.i_aw_bk))
    g(lambda e: e.dma_start(
        out=c.A16[0:1, c.o_rows // 2: c.o_rows // 2 + 5 * D], in_=c.i_rows))
    g(lambda e: e.dma_start(out=c.b16(c.o_gate, 128, KT * E),
                            in_=c.i_gate.rearrange("k p n -> p k n")))
    g(lambda e: e.dma_start(
        out=c.A16[0:E, c.o_fc2b // 2: c.o_fc2b // 2 + 1024], in_=c.i_fc2b))
    p.do("vector", lambda e: e.memset(c.ones, 1.0), inc=c.s_dv[0])
    p.do("vector", lambda e: e.memset(c.b32(c.o_eps, 128, 1), LNEPS), inc=c.s_dv[0])
    c.nc.const_aps.register(F32, LNEPS, c.b32(c.o_eps, 128, 1))


LNB_BYTES = 4096 + 8192 + 320


def _new_lnb(base):
    return {"sq": base, "ytm": base + 4096, "st": base + 12288,
            "war": [None, None, None, None], "tr_war": [None, None]}


def _ln_tile(c, src32, ydst, ti, waits, TR, lnb):
    p = c.p
    pg = ti % 4
    sq = c.b32(lnb["sq"], 128, 1024)
    ytm = c.b16(lnb["ytm"] + pg * 2048, 128, 1024)
    st = c.b32(lnb["st"] + pg * 64, 128, 16)
    w0 = list(waits) + [lnb["war"][pg]]
    a_sq = p.do("scalar", lambda e: e.activation(
        sq, src32, AF.Square, accum_out=st[:, 1:2]), waits=w0, inc=c.s_ac[0])
    p.do("vector", lambda e: e.tensor_scalar(
        src32, src32, 1.0, 0.0, ALU.mult, ALU.add, accum_out=st[:, 0:1]),
        waits=w0)
    p.do("vector", lambda e: e.tensor_scalar(st[:, 2:3], st[:, 0:1], 1.0 / D, None, ALU.mult))
    p.do("vector", lambda e: e.tensor_scalar(st[:, 3:4], st[:, 1:2], 1.0 / D, None, ALU.mult),
         waits=[(c.s_ac[0], a_sq)])
    p.do("vector", lambda e: e.tensor_mul(st[:, 4:5], st[:, 2:3], st[:, 2:3]))
    v_var = p.do("vector", lambda e: e.tensor_tensor(st[:, 5:6], st[:, 3:4], st[:, 4:5], ALU.subtract),
                 inc=c.s_dv[0])
    p.do("scalar", lambda e: e.activation(st[:, 6:7], st[:, 5:6], AF.Ln, bias=LNEPS),
         waits=[(c.s_dv[0], v_var)])
    a_rstd = p.do("scalar", lambda e: e.activation(st[:, 7:8], st[:, 6:7], AF.Exp, scale=-0.5),
                  inc=c.s_ac[0])
    v_nmr = p.do("vector", lambda e: e.tensor_scalar(
        st[:, 8:9], st[:, 2:3], st[:, 7:8], -1.0, ALU.mult, ALU.mult),
        waits=[(c.s_ac[0], a_rstd)], inc=c.s_dv[0])
    a_ytm = p.do("scalar", lambda e: e.activation(
        ytm, src32, AF.Identity, scale=st[:, 7:8], bias=st[:, 8:9]),
        waits=[(c.s_dv[0], v_nmr)], inc=c.s_ac[0])
    last_ev = None
    for j in range(KT):
        sl = TR[j % 2]
        ws = [(c.s_ac[0], a_ytm), lnb["tr_war"][j % 2]]
        v_tr = p.do("tensor", lambda e, j=j, sl=sl: e.transpose(
            c.P16[:, sl:sl + 128], ytm[:, j * 128:(j + 1) * 128], c.ident),
            waits=ws, inc=c.s_pe[0])
        eng = "scalar" if j % 2 == 0 else "vector"
        sem = c.s_ac[0] if j % 2 == 0 else c.s_dv[0]
        if eng == "scalar":
            cp = p.do(eng, lambda e, j=j, sl=sl: e.copy(
                ydst(j), c.P16[:, sl:sl + 128]),
                waits=[(c.s_pe[0], v_tr)], inc=sem)
        else:
            cp = p.do(eng, lambda e, j=j, sl=sl: e.tensor_copy(
                ydst(j), c.P16[:, sl:sl + 128]),
                waits=[(c.s_pe[0], v_tr)], inc=sem)
        lnb["tr_war"][j % 2] = (sem, cp)
        last_ev = (sem, cp)
    lnb["war"][pg] = last_ev
    fin = p.do("scalar", lambda e: e.copy(st[:, 9:10], st[:, 0:1]),
               waits=[last_ev, lnb["tr_war"][0], lnb["tr_war"][1]],
               inc=c.s_ac[0])
    lnb["war"][pg] = (c.s_ac[0], fin)
    return fin


def _ffn(c, xin, xout, fuse_attnb=False):
    p = c.p
    B = c.POOL
    o_xacc = B
    o_y = o_xacc + 65536
    o_a = o_y + 32768
    o_w1 = o_a + 32768
    o_w2 = o_w1 + 8192
    o_comb = o_w2 + 32768
    o_comb16 = _align(o_comb + 960)
    o_combfm = _align(o_comb16 + 1024)
    o_gel = o_combfm + 4096
    o_gt = o_gel + 4096
    o_ge = o_gt + 128
    o_ln = _align(o_ge + 3 * 64)
    assert o_ln + LNB_BYTES <= ARENA_BYTES, o_ln

    c._gate = {"gt": o_gt, "e": o_ge, "comb": o_comb, "c16": o_comb16,
               "cfm": o_combfm,
               "war": {"ps": None, "buf": None, "ct": None}}
    lnb = _new_lnb(o_ln)
    env = dict(o_xacc=o_xacc, o_y=o_y, o_a=o_a, o_w1=o_w1, o_w2=o_w2,
               o_gel=o_gel, lnb=lnb, fuse=fuse_attnb,
               o_wvb=o_y, o_yb=o_y + 16384)
    if not hasattr(c, "_ffn_tr"):
        c._ffn_tr = {"w1_free": [None, None], "w2_free": [None, None],
                     "pairx_free": [None, None], "pairg_free": [None, None],
                     "gel_free": [None, None], "py_free": [None, None]}
    for half in range(2):
        _ffn_half(c, xin, xout, half, env)


F_PX = [0, 1024]
F_PG = [512, 1536]
F_PY = [2048, 3072]
F_TR = [6400, 7168]
F_PGATE = 2048
F_PCT = 2176


def _ffn_half(c, xin, xout, half, env):
    p = c.p
    base_t = half * 16
    s_pe, s_dv, s_ac = c.s_pe, c.s_dv, c.s_ac
    tr = c._ffn_tr
    o_xacc, o_y, o_a, o_w1, o_w2, o_gel = (env[k] for k in
                                           ("o_xacc", "o_y", "o_a", "o_w1", "o_w2", "o_gel"))
    lnb = env["lnb"]
    fuse = env["fuse"]
    o_wvb, o_yb = env["o_wvb"], env["o_yb"]

    xacc = lambda tt: c.b32(o_xacc + tt * 4096, 128, 1024)
    yfm = lambda k, c0, n: c.b16(o_y + k * 4096 + c0 * 2, 128, n)
    a_e = lambda j, c0, n: c.b16(o_a + j * 4096 + c0 * 2, 128, n)
    w1 = lambda sl, c0, n: c.b16(o_w1 + sl * 4096 + c0 * 2, 128, n)
    w2 = lambda sl, c0, n: c.b16(o_w2 + sl * 16384 + c0 * 2, 128, n)
    comb = lambda tt, ei: c.A32[:, (c._gate["comb"] // 4) + tt * 15 + ei:
                                (c._gate["comb"] // 4) + tt * 15 + ei + 1]
    combfm = lambda t0, n: c.A16[0:E, (c._gate["cfm"] // 2) + t0:
                                 (c._gate["cfm"] // 2) + t0 + n]
    gel = lambda sl: c.b32(o_gel + sl * 2048, 128, 512)

    ev_final = {}
    if half == 1:
        p.barrier(c.bar, engines=("sync", "tensor", "vector", "scalar"))

    ev_x, ev_y, ev_comb, ev_cfm = {}, {}, {}, {}
    for tt in range(16):
        gt = base_t + tt
        ev_x[tt] = c.dq_x.issue(lambda e, tt=tt, gt=gt: e.dma_start(
            out=xacc(tt), in_=xin[gt * 128:(gt + 1) * 128, :]))
    ev_c16 = {}
    for tt in range(16):
        ydst = lambda j, tt=tt: yfm(j, tt * 128, 128)
        ev_y[tt] = _ln_tile(c, xacc(tt), ydst, tt, [ev_x[tt]], F_TR, lnb)
        ev_comb[tt], ev_c16[tt] = _gate_tile(c, tt, yfm, comb, combfm, ev_y[tt])
    G2 = c._gate
    ctw = G2["war"]
    for tt in range(16):
        cb16 = c.b16(G2["c16"] + tt * 64, 128, 15)
        ptv = p.do("tensor", lambda e, cb16=cb16: e.transpose(
            c.P16[0:E, 2 * F_PCT: 2 * F_PCT + 128], cb16, c.ident),
            waits=[(s_dv[0], ev_c16[tt]), ctw["ct"]], inc=s_pe[3])
        acv = p.do("scalar", lambda e, tt=tt: e.copy(
            combfm(tt * 128, 128), c.P16[0:E, 2 * F_PCT: 2 * F_PCT + 128]),
            waits=[(s_pe[3], ptv)], inc=s_ac[0])
        ctw["ct"] = (s_ac[0], acv)
        ev_cfm[tt] = acv

    def fc1_like(nj, wsrc, bias_xy):
        ev_a_last = None
        ev_a_c = [None] * 4
        for j in range(nj):
            sl = j % 2
            lwv = c.dq_wl.issue(lambda e, j=j, sl=sl: e.dma_start(
                out=w1(sl, 0, 2048), in_=wsrc(j)),
                waits=[tr["w1_free"][sl]])
            last_pe = None
            for cc in range(4):
                pp = (j * 4 + cc) % 2
                bx, bg = bias_xy(j)
                wts = [lwv, tr["pairx_free"][pp],
                       (c.s_ac[0], ev_y[4 * cc + 3])]
                pe_x = None
                for k in range(KT):
                    pe_x = p.do("tensor", lambda e, k=k, pp=pp, sl=sl, cc=cc: e.matmul(
                        c.P32[:, F_PX[pp]:F_PX[pp] + 512],
                        w1(sl, k * 256, 128), yfm(k, cc * 512, 512),
                        start=(k == 0), stop=(k == KT - 1)),
                        waits=wts if k == 0 else [],
                        inc=s_pe[1] if k == KT - 1 else None)
                wts_g = [tr["pairg_free"][pp]]
                pe_g = None
                for k in range(KT):
                    pe_g = p.do("tensor", lambda e, k=k, pp=pp, sl=sl, cc=cc: e.matmul(
                        c.P32[:, F_PG[pp]:F_PG[pp] + 512],
                        w1(sl, k * 256 + 128, 128), yfm(k, cc * 512, 512),
                        start=(k == 0), stop=(k == KT - 1)),
                        waits=wts_g if k == 0 else [],
                        inc=s_pe[1] if k == KT - 1 else None)
                last_pe = pe_g
                ag = p.do("scalar", lambda e, pp=pp, bg=bg: e.activation(
                    gel(pp), c.P32[:, F_PG[pp]:F_PG[pp] + 512], AF.Gelu, bias=bg),
                    waits=[(s_pe[1], pe_g), tr["gel_free"][pp]], inc=s_ac[1])
                tr["pairg_free"][pp] = (s_ac[1], ag)
                av = p.do("vector", lambda e, j=j, cc=cc, pp=pp, bx=bx: e.scalar_tensor_tensor(
                    a_e(j, cc * 512, 512), c.P32[:, F_PX[pp]:F_PX[pp] + 512],
                    bx, gel(pp), ALU.add, ALU.mult),
                    waits=[(s_pe[1], pe_x), (s_ac[1], ag)], inc=s_dv[1])
                tr["pairx_free"][pp] = (s_dv[1], av)
                tr["gel_free"][pp] = (s_dv[1], av)
                ev_a_last = av
                ev_a_c[cc] = av
            tr["w1_free"][sl] = (s_pe[1], last_pe)
        return ev_a_c, (s_pe[1], s_pe[1].n)

    def fc2_like(jslots, ev_a_c, w2_ready, evac, has_tail=False):
        for tt in range(16):
            py = tt % 2
            wts = [tr["py_free"][py], (s_dv[1], ev_a_c[tt // 4]), w2_ready]
            pv = None
            for n in range(2):
                nj = len(jslots)
                for idx, (jj, sl, col0) in enumerate(jslots):
                    last = (not has_tail) and n == 1 and idx == nj - 1
                    pv2 = p.do("tensor", lambda e, tt=tt, n=n, jj=jj, sl=sl, col0=col0, py=py,
                               st=(idx == 0): e.matmul(
                        c.P32[:, F_PY[py] + n * 512: F_PY[py] + n * 512 + 512],
                        a_e(jj, tt * 128, 128), w2(sl, col0 + n * 512, 512),
                        start=st, stop=False),
                        waits=wts if (n == 0 and idx == 0) else [],
                        inc=s_pe[2] if last else None)
                    if last:
                        pv = pv2
            tr["py_free"][py] = evac(tt, py, pv)

    for ei in range(E):
        esl = ei % 2
        lw2 = None
        for k in range(KT):
            lw2 = c.dq_wl.issue(lambda e, ei=ei, k=k, esl=esl: e.dma_start(
                out=w2(esl, k * 1024, 1024), in_=c.i_fc2S[ei, k]),
                waits=[tr["w2_free"][esl]] if k == 0 else [])
        sc_ev = None
        for k in range(KT):
            sc_ev = p.do("vector", lambda e, ei=ei, k=k, esl=esl: e.tensor_scalar(
                w2(esl, k * 1024, 1024), w2(esl, k * 1024, 1024),
                c.A32[:, (c.o_mult // 4) + ei * 8 + k: (c.o_mult // 4) + ei * 8 + k + 1],
                None, ALU.mult),
                waits=[lw2], inc=s_dv[2])
        ev_a, _ = fc1_like(
            8, wsrc=lambda j, ei=ei: c.i_fc1S[ei, j],
            bias_xy=lambda j, ei=ei: (
                c.A32[:, (c.o_fc1b // 4) + ei * 16 + 2 * j: (c.o_fc1b // 4) + ei * 16 + 2 * j + 1],
                c.A32[:, (c.o_fc1b // 4) + ei * 16 + 2 * j + 1: (c.o_fc1b // 4) + ei * 16 + 2 * j + 2]))

        def evac_moe(tt, py, pv, ei=ei):
            dv = p.do("vector", lambda e, tt=tt, py=py, ei=ei: e.scalar_tensor_tensor(
                xacc(tt), c.P32[:, F_PY[py]:F_PY[py] + 1024],
                comb(tt, ei), xacc(tt), ALU.mult, ALU.add),
                waits=[(s_pe[2], pv), (s_dv[0], ev_comb[tt])], inc=s_dv[3])
            return (s_dv[3], dv)

        fc2_like([(j, esl, j * 1024) for j in range(8)], ev_a,
                 (s_dv[2], sc_ev), evac_moe)
        tr["w2_free"][esl] = (s_dv[3], s_dv[3].n)

    lw2 = None
    for sl in range(2):
        for k in range(KT):
            lw2 = c.dq_wl.issue(lambda e, sl=sl, k=k: e.dma_start(
                out=w2(sl, k * 1024, 1024), in_=c.i_sfc2S[sl * 8 + k]),
                waits=[tr["w2_free"][sl]] if k == 0 else [])
    sc_ev = None
    for sl in range(2):
        for k in range(KT):
            sc_ev = p.do("vector", lambda e, sl=sl, k=k: e.tensor_scalar(
                w2(sl, k * 1024, 1024), w2(sl, k * 1024, 1024),
                c.A32[:, (c.o_smult // 4) + sl * 8 + k: (c.o_smult // 4) + sl * 8 + k + 1],
                None, ALU.mult),
                waits=[lw2], inc=s_dv[2])
    for fh in range(2):
        ev_a, fc1_done = fc1_like(
            8, wsrc=lambda j, fh=fh: c.i_sfc1S[fh * 8 + j],
            bias_xy=lambda j, fh=fh: (
                c.A32[:, (c.o_sfc1b // 4) + 2 * (fh * 8 + j): (c.o_sfc1b // 4) + 2 * (fh * 8 + j) + 1],
                c.A32[:, (c.o_sfc1b // 4) + 2 * (fh * 8 + j) + 1: (c.o_sfc1b // 4) + 2 * (fh * 8 + j) + 2]))

        def evac_sh(tt, py, pv, fh=fh):
            if fh == 0:
                for n in range(2):
                    p.do("tensor", lambda e, py=py, n=n: e.matmul(
                        c.P32[:, F_PY[py] + n * 512:F_PY[py] + n * 512 + 512],
                        c.ones, c.row16(4, n * 512, 512),
                        start=False, stop=False))
                    pv = p.do("tensor", lambda e, tt=tt, py=py, n=n: e.matmul(
                        c.P32[:, F_PY[py] + n * 512:F_PY[py] + n * 512 + 512],
                        combfm(tt * 128, 128),
                        c.A16[0:E, c.o_fc2b // 2 + n * 512: c.o_fc2b // 2 + n * 512 + 512],
                        start=False, stop=(n == 1)),
                        waits=[(c.s_ac[0], ev_cfm[tt])] if n == 0 else [],
                        inc=s_pe[2] if n == 1 else None)
            dv = p.do("vector", lambda e, tt=tt, py=py: e.tensor_add(
                xacc(tt), xacc(tt), c.P32[:, F_PY[py]:F_PY[py] + 1024]),
                waits=[(s_pe[2], pv)], inc=s_dv[3])
            if fh == 1:
                if fuse:
                    ev_final[tt] = (s_dv[3], dv)
                else:
                    gt = base_t + tt
                    c.dq_st.issue(lambda e, tt=tt, gt=gt: e.dma_start(
                        out=xout[gt * 128:(gt + 1) * 128, :], in_=xacc(tt)),
                        waits=[(s_dv[3], dv)])
            return (s_dv[3], dv)

        if fuse and fh == 1:
            lwv = None
            for k in range(KT):
                lwv = c.dq_wl.issue(lambda e, k=k: e.dma_start(
                    out=c.b16(o_wvb + k * 2048, 128, 1024), in_=c.i_bw_vT[k]),
                    waits=[fc1_done] if k == 0 else [])
        fc2_like([(j, fh, j * 1024) for j in range(8)], ev_a,
                 (s_dv[2], sc_ev), evac_sh, has_tail=(fh == 0))
    for sl in range(2):
        tr["w2_free"][sl] = (s_dv[3], s_dv[3].n)

    if fuse:
        TRE = [0, 2048]
        PVE = [1024, 1536]
        epi_lnb = env.setdefault("epi_lnb", _new_lnb(env["lnb"]["sq"]))
        epi_tr = env.setdefault("epi_tr", {"pv_free": [None, None],
                                           "last_copy": None})
        for tt in range(16):
            sl = tt % 2
            ydst = lambda j, sl=sl: c.b16(o_yb + sl * 2048 + j * 256, 128, 128)
            evl = _ln_tile(c, xacc(tt), ydst, tt, [ev_final[tt]], TRE, epi_lnb)
            wts = [(c.s_ac[0], evl), lwv, epi_tr["pv_free"][sl]]
            pvv = None
            for n in range(2):
                for k in range(KT):
                    p.do("tensor", lambda e, k=k, n=n, sl=sl: e.matmul(
                        c.P32[:, PVE[sl] + 0:PVE[sl] + 512] if False else
                        c.P32[:, (PVE[0] if n == 0 else PVE[1]):(PVE[0] if n == 0 else PVE[1]) + 512],
                        c.b16(o_yb + sl * 2048 + k * 256, 128, 128),
                        c.b16(o_wvb + k * 2048 + n * 1024, 128, 512),
                        start=(k == 0), stop=False),
                        waits=wts if (k == 0 and n == 0) else [])
                pvv = p.do("tensor", lambda e, n=n: e.matmul(
                    c.P32[:, (PVE[0] if n == 0 else PVE[1]):(PVE[0] if n == 0 else PVE[1]) + 512],
                    c.ones, c.row16(3, n * 512, 512), start=False, stop=True),
                    inc=s_pe[3] if n == 1 else None)
            dvv = p.do("vector", lambda e, tt=tt: e.tensor_add(
                xacc(tt), xacc(tt), c.P32[:, PVE[0]:PVE[0] + 1024]),
                waits=[(s_pe[3], pvv)], inc=s_dv[3])
            epi_tr["pv_free"][0] = (s_dv[3], dvv)
            epi_tr["pv_free"][1] = (s_dv[3], dvv)
            gt = base_t + tt
            c.dq_st.issue(lambda e, tt=tt, gt=gt: e.dma_start(
                out=xout[gt * 128:(gt + 1) * 128, :], in_=xacc(tt)),
                waits=[(s_dv[3], dvv)])


def _gate_tile(c, tt, yfm, comb, combfm, ev_y):
    p = c.p
    s_pe, s_dv, s_ac = c.s_pe, c.s_dv, c.s_ac
    G = c._gate
    war = G["war"]
    gst = c.b32(G["gt"] + (tt % 2) * 64, 128, 16)
    et = c.b32(G["e"], 128, 15)
    et1 = c.b32(G["e"] + 64, 128, 15)
    km = c.b32(G["e"] + 128, 128, 15)
    gps = c.P32[:, F_PGATE:F_PGATE + 15]
    wts = [(c.s_ac[0], ev_y), war["ps"]]
    pv = None
    for k in range(KT):
        pv = p.do("tensor", lambda e, k=k: e.matmul(
            gps, yfm(k, tt * 128, 128),
            c.A16[:, (c.o_gate // 2) + k * 15: (c.o_gate // 2) + (k + 1) * 15],
            start=(k == 0), stop=(k == KT - 1)),
            waits=wts if k == 0 else [],
            inc=s_pe[3] if k == KT - 1 else None)
    p.do("vector", lambda e: e.reduce_max(gst[:, 0:1], gps, axis=AX.X),
         waits=[(s_pe[3], pv), war["buf"]])
    dv0 = p.do("vector", lambda e: e.tensor_scalar(
        gst[:, 1:2], gst[:, 0:1], -1.0, None, ALU.mult), inc=s_dv[0])
    av = p.do("scalar", lambda e: e.activation(et, gps, AF.Exp, bias=gst[:, 1:2]),
              waits=[(s_dv[0], dv0)], inc=s_ac[0])
    p.do("vector", lambda e: e.tensor_scalar(km, et, 1.0, None, ALU.is_ge),
         waits=[(s_ac[0], av)])
    p.do("vector", lambda e: e.scalar_tensor_tensor(et1, km, -2.0, et, ALU.mult, ALU.add))
    p.do("vector", lambda e: e.reduce_max(gst[:, 3:4], et1, axis=AX.X))
    p.do("vector", lambda e: e.tensor_scalar(km, et1, gst[:, 3:4], None, ALU.is_ge))
    p.do("vector", lambda e: e.scalar_tensor_tensor(et1, km, -2.0, et1, ALU.mult, ALU.add))
    p.do("vector", lambda e: e.reduce_max(gst[:, 4:5], et1, axis=AX.X))
    p.do("vector", lambda e: e.tensor_scalar(km, et, gst[:, 4:5], None, ALU.is_ge))
    p.do("vector", lambda e: e.tensor_mul(km, et, km))
    p.do("vector", lambda e: e.reduce_sum(gst[:, 5:6], km, axis=AX.X))
    p.do("vector", lambda e: e.tensor_scalar(gst[:, 6:7], gst[:, 5:6], 1e-20, None, ALU.add))
    p.do("vector", lambda e: e.reciprocal(gst[:, 7:8], gst[:, 6:7]))
    dvc = p.do("vector", lambda e, tt=tt: e.tensor_scalar(
        c.A32[:, (G["comb"] // 4) + tt * 15: (G["comb"] // 4) + (tt + 1) * 15],
        km, gst[:, 7:8], None, ALU.mult), inc=s_dv[0])
    cb16 = c.b16(G["c16"] + tt * 64, 128, 15)
    dv16 = p.do("vector", lambda e, tt=tt: e.tensor_copy(
        cb16, c.A32[:, (G["comb"] // 4) + tt * 15: (G["comb"] // 4) + (tt + 1) * 15]),
        inc=s_dv[0])
    war["buf"] = (s_dv[0], dv16)
    war["ps"] = (s_dv[0], dv16)
    return dvc, dv16


A_PA = [0, 512]
A_PN = [1024, 1536]
A_PS = [2048, 2560]
A_PP = [1024, 1536]
A_PR = [1024, 1536]
A_TRLN = [6400, 7168]
A_TRO = [0, 1024]


def _attn_a(c, xin, xout):
    p = c.p
    B = c.POOL
    o_w = B
    o_y = o_w + 4 * 16384
    o_q = o_y + 16384
    o_kfm = o_q + 16384
    o_ktm = o_kfm + 16384
    o_vp = o_ktm + 16384
    o_out = o_y
    o_S32 = _align(o_vp + 16640)
    o_S16 = _align(o_S32 + 4160)
    o_atm = _align(o_S16 + 2080)
    o_sp = _align(o_atm + 512)
    o_rec = o_sp + 4096
    o_ofm = _align(o_rec + 128)
    o_x2 = _align(o_ofm + 4096)
    o_nx = o_x2 + 8192
    o_ln = o_nx + 8192
    assert o_ln + LNB_BYTES <= ARENA_BYTES, o_ln

    lnb = _new_lnb(o_ln)
    globals()["ATTN_OFFS"] = dict(o_w=o_w, o_y=o_y, o_q=o_q,
        o_kfm=o_kfm, o_ktm=o_ktm, o_vp=o_vp, o_out=o_out, o_S32=o_S32,
        o_S16=o_S16, o_x2=o_x2, o_nx=o_nx, o_ofm=o_ofm)
    wmat = lambda wi, k, c0, n: c.b16(o_w + wi * 16384 + k * 2048 + c0 * 2, 128, n)
    yq = lambda k, c0, n: c.b16(o_y + k * 2048 + c0 * 2, 128, n)

    lw = {}
    for wi, (name, src) in enumerate((("q", c.i_aw_qT), ("k", c.i_aw_kT),
                                      ("v", c.i_aw_vT), ("o", c.i_aw_oT))):
        for k in range(KT):
            lw[name] = c.dq_wl.issue(lambda e, wi=wi, k=k, src=src: e.dma_start(
                out=c.b16(o_w + wi * 16384 + k * 2048, 128, 1024), in_=src[k]))

    tk = {"sp_free": [None, None], "pp_free": [None, None],
          "pa_free": [None, None], "pn_free": [None, None],
          "ps_free": [None, None], "pr_free": [None, None],
          "rec_free": [None, None], "tro_free": [None, None],
          "x2_free": [None, None], "nx_free": [None, None],
          "atm_free": [None, None],
          "ktm_done": None, "v_done": None, "q_done": None, "kfm_done": None,
          "Sbf": [None] * 16, "Sadd": None, "out_ev": [None] * 8}
    first_state = [True]

    p.do("vector", lambda e: e.memset(
        c.A16[:, o_vp // 2: o_vp // 2 + 8 * 1040].rearrange(
            "p (h z) -> p h z", z=65)[:, :, 64:65], 1.0), inc=c.s_dv[1])

    def proj_phase(qi, prefix, wy, ev_y_q):
        PHX = os.environ.get("KDBG_PH", "all")
        NU = int(os.environ.get("KDBG_NU", "99"))
        NOEV = os.environ.get("KDBG_NOEV", "0") == "1"
        if not prefix:
            for which in ("q", "k"):
                wi = 0 if which == "q" else 1
                boff = c.o_awbq if which == "q" else c.o_awbk
                dsto = o_q if which == "q" else o_kfm
                last = None
                for d in range(8):
                    for c2 in range(2):
                        if d * 2 + c2 >= NU:
                            continue
                        ps = (d * 2 + c2) % 2
                        wts = [(c.s_ac[0], ev_y_q[4 * c2 + 3]), lw[which],
                               tk["pp_free"][ps],
                               tk["pn_free"][0], tk["pn_free"][1],
                               tk["pr_free"][0], tk["pr_free"][1]]
                        pv = None
                        for k in range(KT):
                            pv = p.do("tensor", lambda e, d=d, c2=c2, k=k, ps=ps, wi=wi: e.matmul(
                                c.P32[:, A_PP[ps]:A_PP[ps] + 512],
                                wmat(wi, k, d * 128, 128), yq(k, c2 * 512, 512),
                                start=(k == 0), stop=(k == KT - 1)),
                                waits=wts if k == 0 else [],
                                inc=c.s_pe[1] if k == KT - 1 else None)
                        if NOEV:
                            tk["pp_free"][ps] = (c.s_pe[1], pv)
                            last = None
                            continue
                        EV = os.environ.get("KDBG_EV", "expln")
                        if EV == "cp":
                            av = p.do("scalar", lambda e, ps=ps: e.copy(
                                c.b32(o_sp + ps * 2048, 128, 512),
                                c.P32[:, A_PP[ps]:A_PP[ps] + 512]),
                                waits=[(c.s_pe[1], pv), tk["sp_free"][ps]],
                                inc=c.s_ac[1])
                            tk["sp_free"][ps] = (c.s_ac[1], av)
                            tk["pp_free"][ps] = (c.s_ac[1], av)
                            last = None
                            continue
                        if EV == "exp":
                            av = p.do("scalar", lambda e, ps=ps, d=d, boff=boff: e.activation(
                                c.b32(o_sp + ps * 2048, 128, 512),
                                c.P32[:, A_PP[ps]:A_PP[ps] + 512], AF.Exp,
                                bias=c.A32[:, boff // 4 + d: boff // 4 + d + 1]),
                                waits=[(c.s_pe[1], pv), tk["sp_free"][ps]],
                                inc=c.s_ac[1])
                            tk["sp_free"][ps] = (c.s_ac[1], av)
                            tk["pp_free"][ps] = (c.s_ac[1], av)
                            last = None
                            continue
                        p.do("scalar", lambda e, ps=ps, d=d, boff=boff: e.activation(
                            c.b32(o_sp + ps * 2048, 128, 512),
                            c.P32[:, A_PP[ps]:A_PP[ps] + 512], AF.Exp,
                            bias=c.A32[:, boff // 4 + d: boff // 4 + d + 1]),
                            waits=[(c.s_pe[1], pv), tk["sp_free"][ps]])
                        av = p.do("scalar", lambda e, ps=ps, d=d, c2=c2, dsto=dsto: e.activation(
                            c.b16(dsto + d * 2048 + c2 * 1024, 128, 512),
                            c.b32(o_sp + ps * 2048, 128, 512), AF.Ln, bias=1.0),
                            inc=c.s_ac[1])
                        tk["sp_free"][ps] = (c.s_ac[1], av)
                        tk["pp_free"][ps] = (c.s_ac[1], av)
                        last = av
                if last is not None:
                    tk["q_done" if which == "q" else "kfm_done"] = (c.s_ac[1], last)
        if PHX == "projq":
            return
        uslot = [0]
        for tt in range(8):
            for n in range(2):
                ps = uslot[0] % 2
                uslot[0] += 1
                wts0 = [(c.s_ac[0], ev_y_q[tt]), lw["k"], tk["pp_free"][ps],
                        tk["pn_free"][0], tk["pn_free"][1],
                        tk["pr_free"][0], tk["pr_free"][1]]
                for k in range(KT):
                    p.do("tensor", lambda e, tt=tt, n=n, k=k, ps=ps: e.matmul(
                        c.P32[:, A_PP[ps]:A_PP[ps] + 512],
                        yq(k, tt * 128, 128), wmat(1, k, n * 512, 512),
                        start=(k == 0), stop=False),
                        waits=wts0 if k == 0 else [])
                pv = p.do("tensor", lambda e, n=n, ps=ps: e.matmul(
                    c.P32[:, A_PP[ps]:A_PP[ps] + 512], c.ones,
                    c.row16(0, n * 512, 512), start=False, stop=True),
                    inc=c.s_pe[1])
                p.do("scalar", lambda e, ps=ps: e.activation(
                    c.b32(o_sp + ps * 2048, 128, 512),
                    c.P32[:, A_PP[ps]:A_PP[ps] + 512], AF.Exp),
                    waits=[(c.s_pe[1], pv), tk["sp_free"][ps]])
                av = p.do("scalar", lambda e, tt=tt, n=n, ps=ps: e.activation(
                    c.b16(o_ktm + tt * 2048 + n * 1024, 128, 512),
                    c.b32(o_sp + ps * 2048, 128, 512), AF.Ln, bias=1.0),
                    inc=c.s_ac[1])
                tk["sp_free"][ps] = (c.s_ac[1], av)
                if prefix:
                    dvm = p.do("vector", lambda e, tt=tt, n=n: e.tensor_scalar(
                        c.b16(o_ktm + tt * 2048 + n * 1024, 128, 512),
                        c.b16(o_ktm + tt * 2048 + n * 1024, 128, 512),
                        c.pmask, None, ALU.mult),
                        waits=[(c.s_ac[1], av)], inc=c.s_dv[1])
                    tk["ktm_done"] = (c.s_dv[1], dvm)
                else:
                    tk["ktm_done"] = (c.s_ac[1], av)
                tk["pp_free"][ps] = tk["ktm_done"]
            if PHX == "projk":
                continue
            for n in range(2):
                ps = uslot[0] % 2
                uslot[0] += 1
                wtsv = [(c.s_ac[0], ev_y_q[tt]), lw["v"], tk["pp_free"][ps]]
                for k in range(KT):
                    p.do("tensor", lambda e, tt=tt, n=n, k=k, ps=ps: e.matmul(
                        c.P32[:, A_PP[ps]:A_PP[ps] + 512],
                        yq(k, tt * 128, 128), wmat(2, k, n * 512, 512),
                        start=(k == 0), stop=False),
                        waits=wtsv if k == 0 else [])
                pv = p.do("tensor", lambda e, n=n, ps=ps: e.matmul(
                    c.P32[:, A_PP[ps]:A_PP[ps] + 512], c.ones,
                    c.row16(1, n * 512, 512), start=False, stop=True),
                    inc=c.s_pe[1])
                vpr = c.A16[:, (o_vp // 2) + tt * 1040: (o_vp // 2) + (tt + 1) * 1040].rearrange(
                    "p (h z) -> p h z", z=65)
                dv = p.do("vector", lambda e, n=n, ps=ps, vpr=vpr: e.tensor_copy(
                    vpr[:, n * 8:(n + 1) * 8, 0:64],
                    c.P32[:, A_PP[ps]:A_PP[ps] + 512].rearrange(
                        "p (h z) -> p h z", z=64)),
                    waits=[(c.s_pe[1], pv)], inc=c.s_dv[1])
                tk["pp_free"][ps] = (c.s_dv[1], dv)
                tk["v_done"] = (c.s_dv[1], dv)

    def state_mm(cc, hh, waits, defer=True):
        g = hh // 2
        so = A_PS[g % 2] + 65 * (hh % 2)
        vp = c.A16[:, (o_vp // 2) + cc * 1040 + 65 * hh:
                   (o_vp // 2) + cc * 1040 + 65 * hh + 65]
        kt = c.A16[:, (o_ktm // 2) + cc * 1024 + 64 * hh:
                   (o_ktm // 2) + cc * 1024 + 64 * hh + 64]
        p.do("tensor", lambda e, kt=kt, vp=vp, so=so: e.matmul(
            c.P32[0:64, so:so + 65], kt, vp, start=True, stop=True,
            tile_position=(0, 0)), waits=waits)
        pst = p.do("tensor", lambda e, kt=kt, vp=vp, so=so: e.matmul(
            c.P32[64:128, so:so + 65], kt, vp, start=True,
            stop=True, tile_position=(0, 64)), inc=c.s_pe[2])
        if hh % 2 == 0:
            return None
        h0 = hh - 1
        so0 = A_PS[g % 2]
        if first_state[0]:
            dv = p.do("vector", lambda e, h0=h0, so0=so0: e.tensor_copy(
                c.b32(o_S32 + 65 * h0 * 4, 128, 130),
                c.P32[:, so0:so0 + 130]),
                waits=[(c.s_pe[2], pst)], inc=c.s_dv[1])
        else:
            dv = p.do("vector", lambda e, h0=h0, so0=so0: e.tensor_add(
                c.b32(o_S32 + 65 * h0 * 4, 128, 130),
                c.b32(o_S32 + 65 * h0 * 4, 128, 130),
                c.P32[:, so0:so0 + 130]),
                waits=[(c.s_pe[2], pst)], inc=c.s_dv[1])
        return dv

    def scan_chunk(cc):
        def emit_A(h):
            sl = h % 2
            base = 64 * (h % 2)
            dtile = h // 2
            kf = c.A16[base:base + 64, (o_kfm // 2) + dtile * 1024 + cc * 128:
                       (o_kfm // 2) + dtile * 1024 + cc * 128 + 128]
            qf = c.A16[base:base + 64, (o_q // 2) + dtile * 1024 + cc * 128:
                       (o_q // 2) + dtile * 1024 + cc * 128 + 128]
            pv = p.do("tensor", lambda e, kf=kf, qf=qf, sl=sl: e.matmul(
                c.P32[:, A_PA[sl]:A_PA[sl] + 128], kf, qf, start=True, stop=True),
                waits=[tk["q_done"], tk["kfm_done"], tk["pa_free"][sl]],
                inc=c.s_pe[2])
            atm = c.b16(o_atm + sl * 256, 128, 128)
            dv = p.do("vector", lambda e, sl=sl, atm=atm: e.tensor_tensor(
                atm, c.P32[:, A_PA[sl]:A_PA[sl] + 128], c.trimask, ALU.mult),
                waits=[(c.s_pe[2], pv), tk["atm_free"][sl]], inc=c.s_dv[1])
            tk["pa_free"][sl] = (c.s_dv[1], dv)
            return (c.s_dv[1], dv), atm

        pend = None
        last_ev = None
        for h in range(17):
            cur = emit_A(h) if h < 16 else None
            if pend is not None:
                hh, maskev, atm = pend
                sl = hh % 2
                base = 64 * (hh % 2)
                dtile = hh // 2
                vp = c.A16[:, (o_vp // 2) + cc * 1040 + 65 * hh:
                           (o_vp // 2) + cc * 1040 + 65 * hh + 65]
                qf = c.A16[base:base + 64, (o_q // 2) + dtile * 1024 + cc * 128:
                           (o_q // 2) + dtile * 1024 + cc * 128 + 128]
                sb = c.A16[base:base + 64, (o_S16 // 2) + 65 * hh:
                           (o_S16 // 2) + 65 * hh + 65]
                p.do("tensor", lambda e, atm=atm, vp=vp, sl=sl: e.matmul(
                    c.P32[:, A_PN[sl]:A_PN[sl] + 65], atm, vp,
                    start=True, stop=False),
                    waits=[maskev, tk["v_done"], tk["pn_free"][sl],
                           tk["Sbf"][hh]])
                pnum = p.do("tensor", lambda e, qf=qf, sb=sb, sl=sl: e.matmul(
                    c.P32[:, A_PN[sl]:A_PN[sl] + 65], qf, sb,
                    start=False, stop=True), inc=c.s_pe[2])
                w = ([tk["ktm_done"], tk["ps_free"][(hh // 2) % 2]]
                     if hh % 2 == 0 else [])
                dvst = state_mm(cc, hh, w)
                rec = c.b32(o_rec + sl * 64, 128, 1)
                dvr = p.do("vector", lambda e, rec=rec, sl=sl: e.reciprocal(
                    rec, c.P32[:, A_PN[sl] + 64:A_PN[sl] + 65]),
                    waits=[(c.s_pe[2], pnum), tk["rec_free"][sl]],
                    inc=c.s_dv[1])
                aev = p.do("scalar", lambda e, hh=hh, rec=rec, sl=sl: e.activation(
                    c.b16(o_out + cc * 2048 + 64 * hh * 2, 128, 64),
                    c.P32[:, A_PN[sl]:A_PN[sl] + 64], AF.Copy, scale=rec),
                    waits=[(c.s_dv[1], dvr)], inc=c.s_ac[2])
                tk["rec_free"][sl] = (c.s_ac[2], aev)
                tk["pn_free"][sl] = (c.s_ac[2], aev)
                tk["atm_free"][sl] = (c.s_pe[2], pnum)
                last_ev = aev
                if hh % 2 == 1:
                    h0 = hh - 1
                    dsb = p.do("vector", lambda e, h0=h0: e.tensor_copy(
                        c.b16(o_S16 + 65 * h0 * 2, 128, 130),
                        c.b32(o_S32 + 65 * h0 * 4, 128, 130)), inc=c.s_dv[1])
                    tk["Sbf"][h0] = (c.s_dv[1], dsb)
                    tk["Sbf"][h0 + 1] = (c.s_dv[1], dsb)
                    tk["ps_free"][(hh // 2) % 2] = (c.s_dv[1], dsb)
            pend = (h, cur[0], cur[1]) if h < 16 else None
        tk["out_ev"][cc] = (c.s_ac[2], last_ev)

    def wo_quarter(qi):
        for cc in range(8):
            gt = qi * 8 + cc
            sl = cc % 2
            evx = c.dq_x.issue(lambda e, gt=gt, sl=sl: e.dma_start(
                out=c.b32(o_x2 + sl * 4096, 128, 1024),
                in_=xin[gt * 128:(gt + 1) * 128, :]),
                waits=[tk["x2_free"][sl]])
            for j in range(KT):
                ts = A_TRO[j % 2]
                wts = [tk["out_ev"][cc], tk["tro_free"][j % 2],
                       tk["pa_free"][0], tk["pa_free"][1]]
                pv = p.do("tensor", lambda e, j=j, ts=ts, cc=cc: e.transpose(
                    c.P16[:, ts:ts + 128],
                    c.b16(o_out + cc * 2048 + j * 256, 128, 128), c.ident),
                    waits=wts, inc=c.s_pe[3])
                av = p.do("scalar", lambda e, j=j, ts=ts, sl=sl: e.copy(
                    c.b16(o_ofm + sl * 2048 + j * 256, 128, 128),
                    c.P16[:, ts:ts + 128]),
                    waits=[(c.s_pe[3], pv)], inc=c.s_ac[2])
                tk["tro_free"][j % 2] = (c.s_ac[2], av)
            aofm = tk["tro_free"][(KT - 1) % 2]
            dvn = None
            for n in range(2):
                wts = [aofm, tk["pr_free"][n], tk["pn_free"][n],
                       tk["ps_free"][0], tk["ps_free"][1]]
                for j in range(KT):
                    p.do("tensor", lambda e, j=j, n=n, sl=sl: e.matmul(
                        c.P32[:, A_PR[n]:A_PR[n] + 512],
                        c.b16(o_ofm + sl * 2048 + j * 256, 128, 128),
                        wmat(3, j, n * 512, 512), start=(j == 0), stop=False),
                        waits=wts if j == 0 else [])
                pv = p.do("tensor", lambda e, n=n: e.matmul(
                    c.P32[:, A_PR[n]:A_PR[n] + 512], c.ones,
                    c.row16(2, n * 512, 512), start=False, stop=True),
                    inc=c.s_pe[3])
                dvn = p.do("vector", lambda e, n=n, sl=sl: e.tensor_add(
                    c.b32(o_nx + sl * 4096 + n * 2048, 128, 512),
                    c.b32(o_x2 + sl * 4096 + n * 2048, 128, 512),
                    c.P32[:, A_PR[n]:A_PR[n] + 512]),
                    waits=[(c.s_pe[3], pv), evx, tk["nx_free"][sl]],
                    inc=c.s_dv[2])
                tk["pr_free"][n] = (c.s_dv[2], dvn)
            stv = c.dq_st.issue(lambda e, gt=gt, sl=sl: e.dma_start(
                out=xout[gt * 128:(gt + 1) * 128, :],
                in_=c.b32(o_nx + sl * 4096, 128, 1024)),
                waits=[(c.s_dv[2], dvn)])
            tk["nx_free"][sl] = stv
            tk["x2_free"][sl] = (c.s_dv[2], dvn)

    PH = os.environ.get("KDBG_PH", "all")

    def quarter(qi, prefix):
        xsrc = c.x_pre if prefix else xin
        ev_y = {}
        if os.environ.get("KDBG_LN", "1") == "0":
            mv = p.do("vector", lambda e: e.memset(
                c.b16(o_y, 128, 8192), 0.001), inc=c.s_dv[0])
            acv = p.do("scalar", lambda e: e.copy(
                c.b16(o_y, 128, 2), c.b16(o_y, 128, 2)), inc=c.s_ac[0])
            for tt in range(8):
                ev_y[tt] = acv
            wy = (c.s_ac[0], acv)
            if PH == "ln":
                return
            proj_phase(qi, prefix, wy, ev_y)
            return
        for tt in range(8):
            gt = qi * 8 + tt
            evx = c.dq_x.issue(lambda e, gt=gt, tt=tt, xsrc=xsrc: e.dma_start(
                out=c.b32(o_x2 + (tt % 2) * 4096, 128, 1024),
                in_=xsrc[gt * 128:(gt + 1) * 128, :]),
                waits=[tk["x2_free"][tt % 2]])
            ydst = lambda j, tt=tt: yq(j, tt * 128, 128)
            ev_y[tt] = _ln_tile(c, c.b32(o_x2 + (tt % 2) * 4096, 128, 1024),
                                ydst, tt, [evx], A_TRLN, lnb)
            tk["x2_free"][tt % 2] = (c.s_ac[0], ev_y[tt])
        wy = (c.s_ac[0], ev_y[7])
        if PH == "ln":
            return
        proj_phase(qi, prefix, wy, ev_y)
        if PH.startswith("proj"):
            return
        if prefix:
            for cc in range(8):
                for hh in range(16):
                    g = hh // 2
                    w = ([tk["ktm_done"], tk["v_done"], tk["ps_free"][g % 2]]
                         if hh % 2 == 0 else [])
                    dv = state_mm(cc, hh, w)
                    if dv is not None:
                        tk["ps_free"][g % 2] = (c.s_dv[1], dv)
                        tk["Sadd"] = (c.s_dv[1], dv)
                if first_state[0]:
                    first_state[0] = False
        else:
            for cc in range(8):
                scan_chunk(cc)
            if PH == "scan":
                return
            wo_quarter(qi)

    NPRE = int(os.environ.get("KDBG_PRE", "4"))
    NOWN = int(os.environ.get("KDBG_OWN", "4"))
    for qi in range(NPRE):
        quarter(qi, prefix=True)
    if NPRE > 0 and PH == "all":
        sbv = p.do("vector", lambda e: e.tensor_copy(
            c.b16(o_S16, 128, 1040), c.b32(o_S32, 128, 1040)),
            waits=[tk["Sadd"]], inc=c.s_dv[1])
    else:
        p.do("vector", lambda e: e.memset(c.b32(o_S32, 128, 1040), 0.0))
        sbv = p.do("vector", lambda e: e.memset(c.b16(o_S16, 128, 1040), 0.0),
                   inc=c.s_dv[1])
        first_state[0] = False
    for h in range(16):
        tk["Sbf"][h] = (c.s_dv[1], sbv)
    for qi in range(NOWN):
        quarter(qi, prefix=False)


def _attn_b(c, xin, xout):
    p = c.p
    B = c.POOL
    o_y = B
    o_wv = o_y + 65536
    o_x2 = o_wv + 16384
    o_nx = o_x2 + 8192
    o_ln = o_nx + 8192
    assert o_ln + LNB_BYTES <= ARENA_BYTES

    lnb = _new_lnb(o_ln)
    yfm = lambda k, c0, n: c.b16(o_y + k * 8192 + c0 * 2, 128, n)
    PV = [0, 1024]
    TRLN = [6400, 7168]

    lwv = None
    for k in range(KT):
        lwv = c.dq_wl.issue(lambda e, k=k: e.dma_start(
            out=c.b16(o_wv + k * 2048, 128, 1024), in_=c.i_bw_vT[k]))

    tk = {"x2_free": [None, None], "nx_free": [None, None],
          "pv_free": [None, None]}
    ev_y = {}
    for tt in range(TILES):
        evx = c.dq_x.issue(lambda e, tt=tt: e.dma_start(
            out=c.b32(o_x2 + (tt % 2) * 4096, 128, 1024),
            in_=xin[tt * 128:(tt + 1) * 128, :]),
            waits=[tk["x2_free"][tt % 2]])
        ydst = lambda j, tt=tt: yfm(j, tt * 128, 128)
        ev_y[tt] = _ln_tile(c, c.b32(o_x2 + (tt % 2) * 4096, 128, 1024),
                            ydst, tt, [evx], TRLN, lnb)
        tk["x2_free"][tt % 2] = (c.s_ac[0], ev_y[tt])

    for tt in range(TILES):
        sl = tt % 2
        evx2 = c.dq_x.issue(lambda e, tt=tt, sl=sl: e.dma_start(
            out=c.b32(o_x2 + sl * 4096, 128, 1024),
            in_=xin[tt * 128:(tt + 1) * 128, :]),
            waits=[tk["x2_free"][sl]])
        for n in range(2):
            wts = [(c.s_ac[0], ev_y[tt]), lwv, tk["pv_free"][sl]]
            for k in range(KT):
                p.do("tensor", lambda e, tt=tt, n=n, k=k, sl=sl: e.matmul(
                    c.P32[:, PV[sl] + n * 512:PV[sl] + n * 512 + 512],
                    yfm(k, tt * 128, 128),
                    c.b16(o_wv + k * 2048 + n * 512 * 2, 128, 512),
                    start=(k == 0), stop=False),
                    waits=wts if (k == 0 and n == 0) else [])
            pv = p.do("tensor", lambda e, n=n, sl=sl: e.matmul(
                c.P32[:, PV[sl] + n * 512:PV[sl] + n * 512 + 512],
                c.ones, c.row16(3, n * 512, 512), start=False, stop=True),
                inc=c.s_pe[3] if n == 1 else None)
        dv = p.do("vector", lambda e, sl=sl: e.tensor_add(
            c.b32(o_nx + sl * 4096, 128, 1024),
            c.b32(o_x2 + sl * 4096, 128, 1024),
            c.P32[:, PV[sl]:PV[sl] + 1024]),
            waits=[(c.s_pe[3], pv), evx2, tk["nx_free"][sl]],
            inc=c.s_dv[2])
        tk["pv_free"][sl] = (c.s_dv[2], dv)
        stv = c.dq_st.issue(lambda e, tt=tt, sl=sl: e.dma_start(
            out=xout[tt * 128:(tt + 1) * 128, :],
            in_=c.b32(o_nx + sl * 4096, 128, 1024)),
            waits=[(c.s_dv[2], dv)])
        tk["nx_free"][sl] = stv
        tk["x2_free"][sl] = (c.s_dv[2], dv)


def _stage(inputs):
    f = np.ascontiguousarray
    s = {}
    s["trimask"] = f(np.triu(np.ones((128, 128), dtype=np.float32)))
    s["identb"] = np.eye(128, dtype=ml_dtypes.bfloat16)
    for nm, w in (("aw_qT", inputs["L0_attn_a_Wq"]), ("aw_kT", inputs["L0_attn_a_Wk"]),
                  ("aw_vT", inputs["L0_attn_a_Wv"]), ("aw_oT", inputs["L0_attn_a_Wo"]),
                  ("bw_vT", inputs["L0_attn_b_Wv"])):
        s[nm] = f(np.asarray(w).T.reshape(KT, 128, D))
    s["aw_bq"] = f(np.asarray(inputs["L0_attn_a_bq"]).reshape(KT, 128).T)
    s["aw_bk"] = f(np.asarray(inputs["L0_attn_a_bk"]).reshape(KT, 128).T)
    s["rows"] = f(np.stack([np.asarray(inputs["L0_attn_a_bk"]),
                            np.asarray(inputs["L0_attn_a_bv"]),
                            np.asarray(inputs["L0_attn_a_bo"]),
                            np.asarray(inputs["L0_attn_b_bv"]),
                            np.asarray(inputs["L0_ffn_s_fc2_b"])]).astype(np.float32).reshape(1, 5 * D))
    s["gateS"] = f(np.asarray(inputs["L0_ffn_gate_w"]).T.reshape(KT, 128, E))
    F1 = np.asarray(inputs["L0_ffn_fc1_w"]).transpose(0, 2, 1)
    s["fc1S"] = f(F1.reshape(E, KT, 128, 2, 8, 128).transpose(0, 4, 2, 1, 3, 5).reshape(E, 8, 128, 2048))
    B1 = np.asarray(inputs["L0_ffn_fc1_b"]).reshape(E, 2, 8, 128)
    s["fc1bS"] = f(B1.transpose(3, 0, 2, 1).reshape(128, E * 16))
    s["multS"] = f(np.asarray(inputs["L0_ffn_geglu_mult"]).reshape(E, 8, 128).transpose(2, 0, 1).reshape(128, E * 8))
    s["fc2S"] = f(np.asarray(inputs["L0_ffn_fc2_w"]).transpose(0, 2, 1).reshape(E, KT, 128, 1024))
    s["fc2bS"] = f(np.asarray(inputs["L0_ffn_fc2_b"]).astype(np.float32))
    S1 = np.asarray(inputs["L0_ffn_s_fc1_w"]).T
    s["sfc1S"] = f(S1.reshape(KT, 128, 2, 16, 128).transpose(3, 1, 0, 2, 4).reshape(16, 128, 2048))
    SB = np.asarray(inputs["L0_ffn_s_fc1_b"]).reshape(2, 16, 128)
    s["sfc1bS"] = f(SB.transpose(2, 1, 0).reshape(128, 32))
    s["smultS"] = f(np.asarray(inputs["L0_ffn_s_geglu_mult"]).reshape(16, 128).T)
    s["sfc2S"] = f(np.asarray(inputs["L0_ffn_s_fc2_w"]).T.reshape(16, 128, 1024))
    return s


def kernel(**inputs):
    x = np.asarray(inputs["L0_attn_a_x"], dtype=np.float32)
    shared = _stage(inputs)
    nc = build(NSUB)
    in_maps = []
    for core in range(8):
        b, h = core // 2, core % 2
        m = dict(shared)
        m["x_own"] = np.ascontiguousarray(x[b, h * NT:(h + 1) * NT, :])
        m["x_pre"] = np.ascontiguousarray(x[b, 0:NT, :])
        m["pmask"] = np.full((128, 1), float(h), dtype=np.float32)
        in_maps.append(m)
    res = run_bass_kernel_spmd(nc, in_maps, core_ids=list(range(8)))
    out = np.empty((4, 8192, D), dtype=np.float32)
    for core in range(8):
        b, h = core // 2, core % 2
        out[b, h * NT:(h + 1) * NT, :] = res.results[core]["y"]
    return out


# version 12
# speedup vs baseline: 1.0854x; 1.0004x over previous
import os
import numpy as np
import ml_dtypes
import concourse.bass as bass
import concourse.mybir as mybir
from concourse.bass_utils import run_bass_kernel_spmd

F32 = mybir.dt.float32
BF16 = mybir.dt.bfloat16
U8 = mybir.dt.uint8
AF = mybir.ActivationFunctionType
ALU = mybir.AluOpType
AX = mybir.AxisListType

D = 1024
NT = 4096
TILES = 32
KT = 8
NSUB = 4
E = 15
LNEPS = 1e-5
ARENA_BYTES = 212608

ENGS = ("sync", "tensor", "vector", "scalar", "gpsimd")


class Sem:
    __slots__ = ("name", "n", "h", "dma")

    def __init__(self, name, dma=False):
        self.name, self.n, self.h, self.dma = name, 0, None, dma


class Prog:
    def __init__(self, nc):
        self.nc = nc
        self.ops = {e: [] for e in ENGS}
        self.sems = []
        self.hiwater = {e: {} for e in ENGS}
        self.auto = {}

    def sem(self, name, dma=False):
        s = Sem(name, dma)
        self.sems.append(s)
        return s

    def do(self, eng, fn, waits=(), inc=None, inc_amt=1):
        auto = self.auto.get(eng)
        if auto is not None and fn is not None:
            waits = list(waits) + [(auto, auto.n)]
            if inc is None:
                inc = auto
                inc_amt = 1
            else:
                assert inc is auto, (eng, inc.name)
        ws = []
        hw = self.hiwater[eng]
        for w in waits:
            if w is None:
                continue
            s, v = w
            assert v <= s.n, f"wait {s.name}>={v} but only {s.n} emitted"
            if hw.get(s.name, -1) >= v:
                continue
            hw[s.name] = v
            ws.append((s, v))
        after = None
        if inc is not None:
            inc.n += inc_amt
            after = inc.n
        self.ops[eng].append((ws, fn, inc, inc_amt))
        return after

    def barrier(self, b, engines=ENGS):
        for e in engines:
            auto = self.auto.get(e)
            if auto is not None:
                self.do(e, None, waits=[(auto, auto.n)])
            self.ops[e].append(([], (lambda eng: eng.nop()), b, 1))
            b.n += 1
        tgt = b.n
        for e in engines:
            self.do(e, None, waits=[(b, tgt)])

    def realize(self):
        from contextlib import ExitStack
        nc = self.nc
        with ExitStack() as st:
            for s in self.sems:
                s.h = st.enter_context(nc.semaphore(s.name))
            blk = st.enter_context(nc.Block())
            for e in ENGS:
                ops = self.ops[e]

                def body(eng, ops=ops):
                    for ws, fn, inc, amt in ops:
                        for s, v in ws:
                            eng.wait_ge(s.h, v)
                        if fn is not None:
                            inst = fn(eng)
                            if inc is not None:
                                inst.then_inc(inc.h, amt)
                getattr(blk, e)(body)


class DmaQ:

    def __init__(self, p, name, eng, k=4):
        self.p, self.eng, self.k = p, eng, k
        self.sems = [p.sem(f"{name}{i}") for i in range(k)]
        self.counts = [0] * k
        self.i = 0

    def issue(self, fn, waits=()):
        j = self.i % self.k
        self.i += 1
        s = self.sems[j]
        w = list(waits)
        if self.counts[j] > 0:
            w.append((s, 16 * self.counts[j]))
        self.counts[j] += 1
        self.p.do(self.eng, fn, waits=w, inc=s, inc_amt=16)
        return (s, 16 * self.counts[j])

    def drain_waits(self):
        return [(s, 16 * n) for s, n in zip(self.sems, self.counts) if n]


def _align(x, a=128):
    return (x + a - 1) & ~(a - 1)


class Ctx:
    pass


def build(nsub=NSUB):
    nc = bass.Bass("TRN2", target_bir_lowering=False,
                   detect_race_conditions=False)
    c = Ctx()
    c.nc = nc
    p = Prog(nc)
    c.p = p

    dt = nc.dram_tensor
    c.x_own = dt("x_own", [NT, D], F32, kind="ExternalInput").ap()
    c.x_pre = dt("x_pre", [NT, D], F32, kind="ExternalInput").ap()
    c.i_pmask = dt("pmask", [128, 1], F32, kind="ExternalInput").ap()
    c.i_trimask = dt("trimask", [128, 128], F32, kind="ExternalInput").ap()
    c.i_ident = dt("identb", [128, 128], BF16, kind="ExternalInput").ap()
    for n in ("aw_qT", "aw_kT", "aw_vT", "aw_oT", "bw_vT"):
        setattr(c, "i_" + n, dt(n, [KT, 128, D], F32, kind="ExternalInput").ap())
    c.i_aw_bq = dt("aw_bq", [128, KT], F32, kind="ExternalInput").ap()
    c.i_aw_bk = dt("aw_bk", [128, KT], F32, kind="ExternalInput").ap()
    c.i_rows = dt("rows", [1, 5 * D], F32, kind="ExternalInput").ap()
    c.i_gate = dt("gateS", [KT, 128, E], F32, kind="ExternalInput").ap()
    c.i_fc1S = dt("fc1S", [E, 8, 128, 2048], F32, kind="ExternalInput").ap()
    c.i_fc1b = dt("fc1bS", [128, E * 16], F32, kind="ExternalInput").ap()
    c.i_mult = dt("multS", [128, E * 8], F32, kind="ExternalInput").ap()
    c.i_fc2S = dt("fc2S", [E, 8, 128, 1024], F32, kind="ExternalInput").ap()
    c.i_fc2b = dt("fc2bS", [E, 1024], F32, kind="ExternalInput").ap()
    c.i_sfc1S = dt("sfc1S", [16, 128, 2048], F32, kind="ExternalInput").ap()
    c.i_sfc1b = dt("sfc1bS", [128, 32], F32, kind="ExternalInput").ap()
    c.i_smult = dt("smultS", [128, 16], F32, kind="ExternalInput").ap()
    c.i_sfc2S = dt("sfc2S", [16, 128, 1024], F32, kind="ExternalInput").ap()
    c.y_out = dt("y", [NT, D], F32, kind="ExternalOutput").ap()
    c.xs0 = dt("xs0", [NT, D], F32, kind="Internal").ap()
    c.xs1 = dt("xs1", [NT, D], F32, kind="Internal").ap()

    c.arena = nc.alloc_sbuf_tensor("arena", [128, ARENA_BYTES], U8)
    c.A32 = c.arena.bitcast(F32).ap()
    c.A16 = c.arena.bitcast(BF16).ap()

    def b32(off, parts, n):
        assert off % 4 == 0 and off + 4 * n <= ARENA_BYTES, (off, n)
        return c.A32[0:parts, off // 4: off // 4 + n]

    def b16(off, parts, n):
        assert off % 2 == 0 and off + 2 * n <= ARENA_BYTES, (off, n)
        return c.A16[0:parts, off // 2: off // 2 + n]
    c.b32, c.b16 = b32, b16

    off = 0

    def take(nbytes):
        nonlocal off
        o = off
        off = _align(off + nbytes)
        return o
    c.o_ident = take(128 * 2)
    c.o_trimask = take(128 * 4)
    c.o_pmask = take(64)
    c.o_rows = take(5 * D * 2)
    c.o_ones = take(128 * 2)
    c.o_gate = take(KT * E * 2)
    c.o_fc1b = take(E * 16 * 4)
    c.o_mult = take(E * 8 * 4)
    c.o_sfc1b = take(32 * 4)
    c.o_smult = take(16 * 4)
    c.o_fc2b = take(1024 * 2)
    c.o_awbq = take(KT * 4)
    c.o_awbk = take(KT * 4)
    c.o_eps = take(64)
    c.POOL = _align(off)
    assert c.POOL < 16 * 1024, c.POOL

    c.psum_h = nc.alloc_psum_tensor("psum", [128, 4096], F32)
    c.P32 = c.psum_h.ap()
    c.P16 = c.psum_h.bitcast(BF16).ap()

    c.bar = p.sem("bar")
    c.dq_x = DmaQ(p, "qx", "sync", 4)
    c.dq_st = DmaQ(p, "qs", "sync", 4)
    c.dq_wl = DmaQ(p, "qw", "gpsimd", 4)
    c.s_pe = [p.sem(f"s_pe{i}") for i in range(4)]
    _dv = p.sem("s_dv")
    _ac = p.sem("s_ac")
    c.s_dv = [_dv] * 4
    c.s_ac = [_ac] * 4
    p.auto["vector"] = _dv
    p.auto["scalar"] = _ac

    c.ident = b16(c.o_ident, 128, 128)
    c.trimask = b32(c.o_trimask, 128, 128)
    c.pmask = b32(c.o_pmask, 128, 1)
    c.ones = b16(c.o_ones, 1, 128)

    def row16(i, j0, n):
        base = c.o_rows // 2 + i * D
        return c.A16[0:1, base + j0: base + j0 + n]
    c.row16 = row16

    _preload(c)
    p.barrier(c.bar)

    if nsub == 1:
        _attn_a(c, c.x_own, c.y_out)
        p.barrier(c.bar)
    elif nsub == 2:
        _attn_a(c, c.x_own, c.xs0)
        p.barrier(c.bar)
        _ffn(c, c.xs0, c.y_out, fuse_attnb=False)
        p.barrier(c.bar)
    elif nsub == 3:
        _attn_a(c, c.x_own, c.xs0)
        p.barrier(c.bar)
        _ffn(c, c.xs0, c.y_out, fuse_attnb=True)
        p.barrier(c.bar)
    else:
        _attn_a(c, c.x_own, c.xs0)
        p.barrier(c.bar)
        _ffn(c, c.xs0, c.xs1, fuse_attnb=True)
        p.barrier(c.bar)
        _ffn(c, c.xs1, c.y_out, fuse_attnb=False)
        p.barrier(c.bar)

    p.do("sync", None, waits=c.dq_st.drain_waits())
    p.realize()
    return nc


def _preload(c):
    p = c.p

    def g(fn):
        return c.dq_wl.issue(fn)

    def s(fn):
        return c.dq_x.issue(fn)
    s(lambda e: e.dma_start(out=c.ident, in_=c.i_ident))
    s(lambda e: e.dma_start(out=c.trimask, in_=c.i_trimask))
    s(lambda e: e.dma_start(out=c.pmask, in_=c.i_pmask))
    s(lambda e: e.dma_start(out=c.b32(c.o_fc1b, 128, E * 16), in_=c.i_fc1b))
    s(lambda e: e.dma_start(out=c.b32(c.o_mult, 128, E * 8), in_=c.i_mult))
    s(lambda e: e.dma_start(out=c.b32(c.o_sfc1b, 128, 32), in_=c.i_sfc1b))
    s(lambda e: e.dma_start(out=c.b32(c.o_smult, 128, 16), in_=c.i_smult))
    s(lambda e: e.dma_start(out=c.b32(c.o_awbq, 128, KT), in_=c.i_aw_bq))
    s(lambda e: e.dma_start(out=c.b32(c.o_awbk, 128, KT), in_=c.i_aw_bk))
    g(lambda e: e.dma_start(
        out=c.A16[0:1, c.o_rows // 2: c.o_rows // 2 + 5 * D], in_=c.i_rows))
    g(lambda e: e.dma_start(out=c.b16(c.o_gate, 128, KT * E),
                            in_=c.i_gate.rearrange("k p n -> p k n")))
    g(lambda e: e.dma_start(
        out=c.A16[0:E, c.o_fc2b // 2: c.o_fc2b // 2 + 1024], in_=c.i_fc2b))
    p.do("vector", lambda e: e.memset(c.ones, 1.0), inc=c.s_dv[0])
    p.do("vector", lambda e: e.memset(c.b32(c.o_eps, 128, 1), LNEPS), inc=c.s_dv[0])
    c.nc.const_aps.register(F32, LNEPS, c.b32(c.o_eps, 128, 1))


LNB_BYTES = 4096 + 8192 + 320


def _new_lnb(base):
    return {"sq": base, "ytm": base + 4096, "st": base + 12288,
            "war": [None, None, None, None], "tr_war": [None, None]}


def _ln_tile(c, src32, ydst, ti, waits, TR, lnb):
    p = c.p
    pg = ti % 4
    sq = c.b32(lnb["sq"], 128, 1024)
    ytm = c.b16(lnb["ytm"] + pg * 2048, 128, 1024)
    st = c.b32(lnb["st"] + pg * 64, 128, 16)
    w0 = list(waits) + [lnb["war"][pg]]
    a_sq = p.do("scalar", lambda e: e.activation(
        sq, src32, AF.Square, accum_out=st[:, 1:2]), waits=w0, inc=c.s_ac[0])
    p.do("vector", lambda e: e.tensor_scalar(
        src32, src32, 1.0, 0.0, ALU.mult, ALU.add, accum_out=st[:, 0:1]),
        waits=w0)
    p.do("vector", lambda e: e.tensor_scalar(st[:, 2:3], st[:, 0:1], 1.0 / D, None, ALU.mult))
    p.do("vector", lambda e: e.tensor_scalar(st[:, 3:4], st[:, 1:2], 1.0 / D, None, ALU.mult),
         waits=[(c.s_ac[0], a_sq)])
    p.do("vector", lambda e: e.tensor_mul(st[:, 4:5], st[:, 2:3], st[:, 2:3]))
    v_var = p.do("vector", lambda e: e.tensor_tensor(st[:, 5:6], st[:, 3:4], st[:, 4:5], ALU.subtract),
                 inc=c.s_dv[0])
    p.do("scalar", lambda e: e.activation(st[:, 6:7], st[:, 5:6], AF.Ln, bias=LNEPS),
         waits=[(c.s_dv[0], v_var)])
    a_rstd = p.do("scalar", lambda e: e.activation(st[:, 7:8], st[:, 6:7], AF.Exp, scale=-0.5),
                  inc=c.s_ac[0])
    v_nmr = p.do("vector", lambda e: e.tensor_scalar(
        st[:, 8:9], st[:, 2:3], st[:, 7:8], -1.0, ALU.mult, ALU.mult),
        waits=[(c.s_ac[0], a_rstd)], inc=c.s_dv[0])
    a_ytm = p.do("scalar", lambda e: e.activation(
        ytm, src32, AF.Identity, scale=st[:, 7:8], bias=st[:, 8:9]),
        waits=[(c.s_dv[0], v_nmr)], inc=c.s_ac[0])
    last_ev = None
    for j in range(KT):
        sl = TR[j % 2]
        ws = [(c.s_ac[0], a_ytm), lnb["tr_war"][j % 2]]
        v_tr = p.do("tensor", lambda e, j=j, sl=sl: e.transpose(
            c.P16[:, sl:sl + 128], ytm[:, j * 128:(j + 1) * 128], c.ident),
            waits=ws, inc=c.s_pe[0])
        eng = "scalar" if j % 2 == 0 else "vector"
        sem = c.s_ac[0] if j % 2 == 0 else c.s_dv[0]
        if eng == "scalar":
            cp = p.do(eng, lambda e, j=j, sl=sl: e.copy(
                ydst(j), c.P16[:, sl:sl + 128]),
                waits=[(c.s_pe[0], v_tr)], inc=sem)
        else:
            cp = p.do(eng, lambda e, j=j, sl=sl: e.tensor_copy(
                ydst(j), c.P16[:, sl:sl + 128]),
                waits=[(c.s_pe[0], v_tr)], inc=sem)
        lnb["tr_war"][j % 2] = (sem, cp)
        last_ev = (sem, cp)
    lnb["war"][pg] = last_ev
    fin = p.do("scalar", lambda e: e.copy(st[:, 9:10], st[:, 0:1]),
               waits=[last_ev, lnb["tr_war"][0], lnb["tr_war"][1]],
               inc=c.s_ac[0])
    lnb["war"][pg] = (c.s_ac[0], fin)
    return fin


def _ffn(c, xin, xout, fuse_attnb=False):
    p = c.p
    B = c.POOL
    o_xacc = B
    o_y = o_xacc + 65536
    o_a = o_y + 32768
    o_w1 = o_a + 32768
    o_w2 = o_w1 + 8192
    o_comb = o_w2 + 32768
    o_comb16 = _align(o_comb + 960)
    o_combfm = _align(o_comb16 + 1024)
    o_gel = o_combfm + 4096
    o_gt = o_gel + 4096
    o_ge = o_gt + 128
    o_ln = _align(o_ge + 3 * 64)
    assert o_ln + LNB_BYTES <= ARENA_BYTES, o_ln

    c._gate = {"gt": o_gt, "e": o_ge, "comb": o_comb, "c16": o_comb16,
               "cfm": o_combfm,
               "war": {"ps": None, "buf": None, "ct": None}}
    lnb = _new_lnb(o_ln)
    env = dict(o_xacc=o_xacc, o_y=o_y, o_a=o_a, o_w1=o_w1, o_w2=o_w2,
               o_gel=o_gel, lnb=lnb, fuse=fuse_attnb,
               o_wvb=o_y, o_yb=o_y + 16384)
    if not hasattr(c, "_ffn_tr"):
        c._ffn_tr = {"w1_free": [None, None], "w2_free": [None, None],
                     "pairx_free": [None, None], "pairg_free": [None, None],
                     "gel_free": [None, None], "py_free": [None, None]}
    for half in range(2):
        _ffn_half(c, xin, xout, half, env)


F_PX = [0, 1024]
F_PG = [512, 1536]
F_PY = [2048, 3072]
F_TR = [6400, 7168]
F_PGATE = 2048
F_PCT = 2176


def _ffn_half(c, xin, xout, half, env):
    p = c.p
    base_t = half * 16
    s_pe, s_dv, s_ac = c.s_pe, c.s_dv, c.s_ac
    tr = c._ffn_tr
    o_xacc, o_y, o_a, o_w1, o_w2, o_gel = (env[k] for k in
                                           ("o_xacc", "o_y", "o_a", "o_w1", "o_w2", "o_gel"))
    lnb = env["lnb"]
    fuse = env["fuse"]
    o_wvb, o_yb = env["o_wvb"], env["o_yb"]

    xacc = lambda tt: c.b32(o_xacc + tt * 4096, 128, 1024)
    yfm = lambda k, c0, n: c.b16(o_y + k * 4096 + c0 * 2, 128, n)
    a_e = lambda j, c0, n: c.b16(o_a + j * 4096 + c0 * 2, 128, n)
    w1 = lambda sl, c0, n: c.b16(o_w1 + sl * 4096 + c0 * 2, 128, n)
    w2 = lambda sl, c0, n: c.b16(o_w2 + sl * 16384 + c0 * 2, 128, n)
    comb = lambda tt, ei: c.A32[:, (c._gate["comb"] // 4) + tt * 15 + ei:
                                (c._gate["comb"] // 4) + tt * 15 + ei + 1]
    combfm = lambda t0, n: c.A16[0:E, (c._gate["cfm"] // 2) + t0:
                                 (c._gate["cfm"] // 2) + t0 + n]
    gel = lambda sl: c.b32(o_gel + sl * 2048, 128, 512)

    ev_final = {}
    if half == 1:
        if fuse:
            p.do("tensor", None, waits=[(s_dv[3], s_dv[3].n),
                                        (s_dv[0], s_dv[0].n),
                                        (c.s_ac[0], c.s_ac[0].n),
                                        (c.s_ac[2], c.s_ac[2].n)])
        else:
            p.barrier(c.bar, engines=("sync", "tensor", "vector", "scalar"))

    ev_x, ev_y, ev_comb, ev_cfm = {}, {}, {}, {}
    epi_st = env.get("epi_store", {})
    for tt in range(16):
        gt = base_t + tt
        ld_w = [epi_st[tt]] if (half == 1 and fuse and tt in epi_st) else []
        ev_x[tt] = c.dq_x.issue(lambda e, tt=tt, gt=gt: e.dma_start(
            out=xacc(tt), in_=xin[gt * 128:(gt + 1) * 128, :]), waits=ld_w)
    ev_c16 = {}
    for tt in range(16):
        ydst = lambda j, tt=tt: yfm(j, tt * 128, 128)
        ev_y[tt] = _ln_tile(c, xacc(tt), ydst, tt, [ev_x[tt]], F_TR, lnb)
        ev_comb[tt], ev_c16[tt] = _gate_tile(c, tt, yfm, comb, combfm, ev_y[tt])
    G2 = c._gate
    ctw = G2["war"]
    for tt in range(16):
        cb16 = c.b16(G2["c16"] + tt * 64, 128, 15)
        ptv = p.do("tensor", lambda e, cb16=cb16: e.transpose(
            c.P16[0:E, 2 * F_PCT: 2 * F_PCT + 128], cb16, c.ident),
            waits=[(s_dv[0], ev_c16[tt]), ctw["ct"]], inc=s_pe[3])
        acv = p.do("scalar", lambda e, tt=tt: e.copy(
            combfm(tt * 128, 128), c.P16[0:E, 2 * F_PCT: 2 * F_PCT + 128]),
            waits=[(s_pe[3], ptv)], inc=s_ac[0])
        ctw["ct"] = (s_ac[0], acv)
        ev_cfm[tt] = acv

    def fc1_like(nj, wsrc, bias_xy):
        ev_a_last = None
        ev_a_c = [None] * 4
        for j in range(nj):
            sl = j % 2
            lwv = c.dq_wl.issue(lambda e, j=j, sl=sl: e.dma_start(
                out=w1(sl, 0, 2048), in_=wsrc(j)),
                waits=[tr["w1_free"][sl]])
            last_pe = None
            for cc in range(4):
                pp = (j * 4 + cc) % 2
                bx, bg = bias_xy(j)
                wts = [lwv, tr["pairx_free"][pp],
                       (c.s_ac[0], ev_y[4 * cc + 3])]
                pe_x = None
                for k in range(KT):
                    pe_x = p.do("tensor", lambda e, k=k, pp=pp, sl=sl, cc=cc: e.matmul(
                        c.P32[:, F_PX[pp]:F_PX[pp] + 512],
                        w1(sl, k * 256, 128), yfm(k, cc * 512, 512),
                        start=(k == 0), stop=(k == KT - 1)),
                        waits=wts if k == 0 else [],
                        inc=s_pe[1] if k == KT - 1 else None)
                wts_g = [tr["pairg_free"][pp]]
                pe_g = None
                for k in range(KT):
                    pe_g = p.do("tensor", lambda e, k=k, pp=pp, sl=sl, cc=cc: e.matmul(
                        c.P32[:, F_PG[pp]:F_PG[pp] + 512],
                        w1(sl, k * 256 + 128, 128), yfm(k, cc * 512, 512),
                        start=(k == 0), stop=(k == KT - 1)),
                        waits=wts_g if k == 0 else [],
                        inc=s_pe[1] if k == KT - 1 else None)
                last_pe = pe_g
                ag = p.do("scalar", lambda e, pp=pp, bg=bg: e.activation(
                    gel(pp), c.P32[:, F_PG[pp]:F_PG[pp] + 512], AF.Gelu, bias=bg),
                    waits=[(s_pe[1], pe_g), tr["gel_free"][pp]], inc=s_ac[1])
                tr["pairg_free"][pp] = (s_ac[1], ag)
                av = p.do("vector", lambda e, j=j, cc=cc, pp=pp, bx=bx: e.scalar_tensor_tensor(
                    a_e(j, cc * 512, 512), c.P32[:, F_PX[pp]:F_PX[pp] + 512],
                    bx, gel(pp), ALU.add, ALU.mult),
                    waits=[(s_pe[1], pe_x), (s_ac[1], ag)], inc=s_dv[1])
                tr["pairx_free"][pp] = (s_dv[1], av)
                tr["gel_free"][pp] = (s_dv[1], av)
                ev_a_last = av
                ev_a_c[cc] = av
            tr["w1_free"][sl] = (s_pe[1], last_pe)
        return ev_a_c, (s_pe[1], s_pe[1].n)

    def fc2_like(jslots, ev_a_c, w2_ready, evac, has_tail=False):
        for tt in range(16):
            py = tt % 2
            wts = [tr["py_free"][py], (s_dv[1], ev_a_c[tt // 4]), w2_ready]
            pv = None
            for n in range(2):
                nj = len(jslots)
                for idx, (jj, sl, col0) in enumerate(jslots):
                    last = (not has_tail) and n == 1 and idx == nj - 1
                    pv2 = p.do("tensor", lambda e, tt=tt, n=n, jj=jj, sl=sl, col0=col0, py=py,
                               st=(idx == 0): e.matmul(
                        c.P32[:, F_PY[py] + n * 512: F_PY[py] + n * 512 + 512],
                        a_e(jj, tt * 128, 128), w2(sl, col0 + n * 512, 512),
                        start=st, stop=False),
                        waits=wts if (n == 0 and idx == 0) else [],
                        inc=s_pe[2] if last else None)
                    if last:
                        pv = pv2
            tr["py_free"][py] = evac(tt, py, pv)

    for ei in range(E):
        esl = ei % 2
        lw2 = None
        for k in range(KT):
            lw2 = c.dq_wl.issue(lambda e, ei=ei, k=k, esl=esl: e.dma_start(
                out=w2(esl, k * 1024, 1024), in_=c.i_fc2S[ei, k]),
                waits=[tr["w2_free"][esl]] if k == 0 else [])
        sc_ev = None
        for k in range(KT):
            sc_ev = p.do("vector", lambda e, ei=ei, k=k, esl=esl: e.tensor_scalar(
                w2(esl, k * 1024, 1024), w2(esl, k * 1024, 1024),
                c.A32[:, (c.o_mult // 4) + ei * 8 + k: (c.o_mult // 4) + ei * 8 + k + 1],
                None, ALU.mult),
                waits=[lw2], inc=s_dv[2])
        ev_a, _ = fc1_like(
            8, wsrc=lambda j, ei=ei: c.i_fc1S[ei, j],
            bias_xy=lambda j, ei=ei: (
                c.A32[:, (c.o_fc1b // 4) + ei * 16 + 2 * j: (c.o_fc1b // 4) + ei * 16 + 2 * j + 1],
                c.A32[:, (c.o_fc1b // 4) + ei * 16 + 2 * j + 1: (c.o_fc1b // 4) + ei * 16 + 2 * j + 2]))

        def evac_moe(tt, py, pv, ei=ei):
            dv = p.do("vector", lambda e, tt=tt, py=py, ei=ei: e.scalar_tensor_tensor(
                xacc(tt), c.P32[:, F_PY[py]:F_PY[py] + 1024],
                comb(tt, ei), xacc(tt), ALU.mult, ALU.add),
                waits=[(s_pe[2], pv), (s_dv[0], ev_comb[tt])], inc=s_dv[3])
            return (s_dv[3], dv)

        fc2_like([(j, esl, j * 1024) for j in range(8)], ev_a,
                 (s_dv[2], sc_ev), evac_moe)
        tr["w2_free"][esl] = (s_dv[3], s_dv[3].n)

    lw2 = None
    for sl in range(2):
        for k in range(KT):
            lw2 = c.dq_wl.issue(lambda e, sl=sl, k=k: e.dma_start(
                out=w2(sl, k * 1024, 1024), in_=c.i_sfc2S[sl * 8 + k]),
                waits=[tr["w2_free"][sl]] if k == 0 else [])
    sc_ev = None
    for sl in range(2):
        for k in range(KT):
            sc_ev = p.do("vector", lambda e, sl=sl, k=k: e.tensor_scalar(
                w2(sl, k * 1024, 1024), w2(sl, k * 1024, 1024),
                c.A32[:, (c.o_smult // 4) + sl * 8 + k: (c.o_smult // 4) + sl * 8 + k + 1],
                None, ALU.mult),
                waits=[lw2], inc=s_dv[2])
    for fh in range(2):
        ev_a, fc1_done = fc1_like(
            8, wsrc=lambda j, fh=fh: c.i_sfc1S[fh * 8 + j],
            bias_xy=lambda j, fh=fh: (
                c.A32[:, (c.o_sfc1b // 4) + 2 * (fh * 8 + j): (c.o_sfc1b // 4) + 2 * (fh * 8 + j) + 1],
                c.A32[:, (c.o_sfc1b // 4) + 2 * (fh * 8 + j) + 1: (c.o_sfc1b // 4) + 2 * (fh * 8 + j) + 2]))

        def evac_sh(tt, py, pv, fh=fh):
            if fh == 0:
                for n in range(2):
                    p.do("tensor", lambda e, py=py, n=n: e.matmul(
                        c.P32[:, F_PY[py] + n * 512:F_PY[py] + n * 512 + 512],
                        c.ones, c.row16(4, n * 512, 512),
                        start=False, stop=False))
                    pv = p.do("tensor", lambda e, tt=tt, py=py, n=n: e.matmul(
                        c.P32[:, F_PY[py] + n * 512:F_PY[py] + n * 512 + 512],
                        combfm(tt * 128, 128),
                        c.A16[0:E, c.o_fc2b // 2 + n * 512: c.o_fc2b // 2 + n * 512 + 512],
                        start=False, stop=(n == 1)),
                        waits=[(c.s_ac[0], ev_cfm[tt])] if n == 0 else [],
                        inc=s_pe[2] if n == 1 else None)
            dv = p.do("vector", lambda e, tt=tt, py=py: e.tensor_add(
                xacc(tt), xacc(tt), c.P32[:, F_PY[py]:F_PY[py] + 1024]),
                waits=[(s_pe[2], pv)], inc=s_dv[3])
            if fh == 1:
                if fuse:
                    ev_final[tt] = (s_dv[3], dv)
                else:
                    gt = base_t + tt
                    c.dq_st.issue(lambda e, tt=tt, gt=gt: e.dma_start(
                        out=xout[gt * 128:(gt + 1) * 128, :], in_=xacc(tt)),
                        waits=[(s_dv[3], dv)])
            return (s_dv[3], dv)

        if fuse and fh == 1:
            lwv = None
            for k in range(KT):
                lwv = c.dq_wl.issue(lambda e, k=k: e.dma_start(
                    out=c.b16(o_wvb + k * 2048, 128, 1024), in_=c.i_bw_vT[k]),
                    waits=[fc1_done] if k == 0 else [])
        fc2_like([(j, fh, j * 1024) for j in range(8)], ev_a,
                 (s_dv[2], sc_ev), evac_sh, has_tail=(fh == 0))
    for sl in range(2):
        tr["w2_free"][sl] = (s_dv[3], s_dv[3].n)

    if fuse:
        TRE = [0, 2048]
        PVE = [1024, 1536]
        epi_lnb = env["lnb"]
        epi_tr = env.setdefault("epi_tr", {"pv_free": [None, None],
                                           "last_copy": None})
        for tt in range(16):
            sl = tt % 2
            ydst = lambda j, sl=sl: c.b16(o_yb + sl * 2048 + j * 256, 128, 128)
            evl = _ln_tile(c, xacc(tt), ydst, tt, [ev_final[tt]], TRE, epi_lnb)
            wts = [(c.s_ac[0], evl), lwv, epi_tr["pv_free"][sl]]
            pvv = None
            for n in range(2):
                for k in range(KT):
                    p.do("tensor", lambda e, k=k, n=n, sl=sl: e.matmul(
                        c.P32[:, PVE[sl] + 0:PVE[sl] + 512] if False else
                        c.P32[:, (PVE[0] if n == 0 else PVE[1]):(PVE[0] if n == 0 else PVE[1]) + 512],
                        c.b16(o_yb + sl * 2048 + k * 256, 128, 128),
                        c.b16(o_wvb + k * 2048 + n * 1024, 128, 512),
                        start=(k == 0), stop=False),
                        waits=wts if (k == 0 and n == 0) else [])
                pvv = p.do("tensor", lambda e, n=n: e.matmul(
                    c.P32[:, (PVE[0] if n == 0 else PVE[1]):(PVE[0] if n == 0 else PVE[1]) + 512],
                    c.ones, c.row16(3, n * 512, 512), start=False, stop=True),
                    inc=s_pe[3] if n == 1 else None)
            dvv = p.do("vector", lambda e, tt=tt: e.tensor_add(
                xacc(tt), xacc(tt), c.P32[:, PVE[0]:PVE[0] + 1024]),
                waits=[(s_pe[3], pvv)], inc=s_dv[3])
            epi_tr["pv_free"][0] = (s_dv[3], dvv)
            epi_tr["pv_free"][1] = (s_dv[3], dvv)
            gt = base_t + tt
            env.setdefault("epi_store", {})[tt] = c.dq_st.issue(
                lambda e, tt=tt, gt=gt: e.dma_start(
                    out=xout[gt * 128:(gt + 1) * 128, :], in_=xacc(tt)),
                waits=[(s_dv[3], dvv)])


def _gate_tile(c, tt, yfm, comb, combfm, ev_y):
    p = c.p
    s_pe, s_dv, s_ac = c.s_pe, c.s_dv, c.s_ac
    G = c._gate
    war = G["war"]
    gst = c.b32(G["gt"] + (tt % 2) * 64, 128, 16)
    et = c.b32(G["e"], 128, 15)
    et1 = c.b32(G["e"] + 64, 128, 15)
    km = c.b32(G["e"] + 128, 128, 15)
    gps = c.P32[:, F_PGATE:F_PGATE + 15]
    wts = [(c.s_ac[0], ev_y), war["ps"]]
    pv = None
    for k in range(KT):
        pv = p.do("tensor", lambda e, k=k: e.matmul(
            gps, yfm(k, tt * 128, 128),
            c.A16[:, (c.o_gate // 2) + k * 15: (c.o_gate // 2) + (k + 1) * 15],
            start=(k == 0), stop=(k == KT - 1)),
            waits=wts if k == 0 else [],
            inc=s_pe[3] if k == KT - 1 else None)
    p.do("vector", lambda e: e.reduce_max(gst[:, 0:1], gps, axis=AX.X),
         waits=[(s_pe[3], pv), war["buf"]])
    dv0 = p.do("vector", lambda e: e.tensor_scalar(
        gst[:, 1:2], gst[:, 0:1], -1.0, None, ALU.mult), inc=s_dv[0])
    av = p.do("scalar", lambda e: e.activation(et, gps, AF.Exp, bias=gst[:, 1:2]),
              waits=[(s_dv[0], dv0)], inc=s_ac[0])
    p.do("vector", lambda e: e.tensor_scalar(km, et, 1.0, None, ALU.is_ge),
         waits=[(s_ac[0], av)])
    p.do("vector", lambda e: e.scalar_tensor_tensor(et1, km, -2.0, et, ALU.mult, ALU.add))
    p.do("vector", lambda e: e.reduce_max(gst[:, 3:4], et1, axis=AX.X))
    p.do("vector", lambda e: e.tensor_scalar(km, et1, gst[:, 3:4], None, ALU.is_ge))
    p.do("vector", lambda e: e.scalar_tensor_tensor(et1, km, -2.0, et1, ALU.mult, ALU.add))
    p.do("vector", lambda e: e.reduce_max(gst[:, 4:5], et1, axis=AX.X))
    p.do("vector", lambda e: e.tensor_scalar(km, et, gst[:, 4:5], None, ALU.is_ge))
    p.do("vector", lambda e: e.tensor_mul(km, et, km))
    p.do("vector", lambda e: e.reduce_sum(gst[:, 5:6], km, axis=AX.X))
    p.do("vector", lambda e: e.tensor_scalar(gst[:, 6:7], gst[:, 5:6], 1e-20, None, ALU.add))
    p.do("vector", lambda e: e.reciprocal(gst[:, 7:8], gst[:, 6:7]))
    dvc = p.do("vector", lambda e, tt=tt: e.tensor_scalar(
        c.A32[:, (G["comb"] // 4) + tt * 15: (G["comb"] // 4) + (tt + 1) * 15],
        km, gst[:, 7:8], None, ALU.mult), inc=s_dv[0])
    cb16 = c.b16(G["c16"] + tt * 64, 128, 15)
    dv16 = p.do("vector", lambda e, tt=tt: e.tensor_copy(
        cb16, c.A32[:, (G["comb"] // 4) + tt * 15: (G["comb"] // 4) + (tt + 1) * 15]),
        inc=s_dv[0])
    war["buf"] = (s_dv[0], dv16)
    war["ps"] = (s_dv[0], dv16)
    return dvc, dv16


A_PA = [0, 512]
A_PN = [1024, 1536]
A_PS = [2048, 2560]
A_PP = [1024, 1536]
A_PR = [1024, 1536]
A_TRLN = [6400, 7168]
A_TRO = [0, 1024]


def _attn_a(c, xin, xout):
    p = c.p
    B = c.POOL
    o_w = B
    o_y = o_w + 4 * 16384
    o_q = o_y + 16384
    o_kfm = o_q + 16384
    o_ktm = o_kfm + 16384
    o_vp = o_ktm + 16384
    o_out = o_y
    o_S32 = _align(o_vp + 16640)
    o_S16 = _align(o_S32 + 4160)
    o_atm = _align(o_S16 + 2080)
    o_sp = _align(o_atm + 512)
    o_rec = o_sp + 4096
    o_ofm = _align(o_rec + 128)
    o_x2 = _align(o_ofm + 4096)
    o_nx = o_x2 + 8192
    o_ln = o_nx + 8192
    assert o_ln + LNB_BYTES <= ARENA_BYTES, o_ln

    lnb = _new_lnb(o_ln)
    globals()["ATTN_OFFS"] = dict(o_w=o_w, o_y=o_y, o_q=o_q,
        o_kfm=o_kfm, o_ktm=o_ktm, o_vp=o_vp, o_out=o_out, o_S32=o_S32,
        o_S16=o_S16, o_x2=o_x2, o_nx=o_nx, o_ofm=o_ofm)
    wmat = lambda wi, k, c0, n: c.b16(o_w + wi * 16384 + k * 2048 + c0 * 2, 128, n)
    yq = lambda k, c0, n: c.b16(o_y + k * 2048 + c0 * 2, 128, n)

    lw = {}
    for wi, (name, src) in enumerate((("q", c.i_aw_qT), ("k", c.i_aw_kT),
                                      ("v", c.i_aw_vT), ("o", c.i_aw_oT))):
        for k in range(KT):
            lw[name] = c.dq_wl.issue(lambda e, wi=wi, k=k, src=src: e.dma_start(
                out=c.b16(o_w + wi * 16384 + k * 2048, 128, 1024), in_=src[k]))

    tk = {"sp_free": [None, None], "pp_free": [None, None],
          "pa_free": [None, None], "pn_free": [None, None],
          "ps_free": [None, None], "pr_free": [None, None],
          "rec_free": [None, None], "tro_free": [None, None],
          "x2_free": [None, None], "nx_free": [None, None],
          "atm_free": [None, None],
          "ktm_done": None, "v_done": None, "q_done": None, "kfm_done": None,
          "Sbf": [None] * 16, "Sadd": None, "out_ev": [None] * 8}
    first_state = [True]

    p.do("vector", lambda e: e.memset(
        c.A16[:, o_vp // 2: o_vp // 2 + 8 * 1040].rearrange(
            "p (h z) -> p h z", z=65)[:, :, 64:65], 1.0), inc=c.s_dv[1])

    def proj_phase(qi, prefix, wy, ev_y_q):
        PHX = os.environ.get("KDBG_PH", "all")
        NU = int(os.environ.get("KDBG_NU", "99"))
        NOEV = os.environ.get("KDBG_NOEV", "0") == "1"
        if not prefix:
            for which in ("q", "k"):
                wi = 0 if which == "q" else 1
                boff = c.o_awbq if which == "q" else c.o_awbk
                dsto = o_q if which == "q" else o_kfm
                last = None
                for d in range(8):
                    for c2 in range(2):
                        if d * 2 + c2 >= NU:
                            continue
                        ps = (d * 2 + c2) % 2
                        wts = [(c.s_ac[0], ev_y_q[4 * c2 + 3]), lw[which],
                               tk["pp_free"][ps],
                               tk["pn_free"][0], tk["pn_free"][1],
                               tk["pr_free"][0], tk["pr_free"][1]]
                        pv = None
                        for k in range(KT):
                            pv = p.do("tensor", lambda e, d=d, c2=c2, k=k, ps=ps, wi=wi: e.matmul(
                                c.P32[:, A_PP[ps]:A_PP[ps] + 512],
                                wmat(wi, k, d * 128, 128), yq(k, c2 * 512, 512),
                                start=(k == 0), stop=(k == KT - 1)),
                                waits=wts if k == 0 else [],
                                inc=c.s_pe[1] if k == KT - 1 else None)
                        if NOEV:
                            tk["pp_free"][ps] = (c.s_pe[1], pv)
                            last = None
                            continue
                        EV = os.environ.get("KDBG_EV", "expln")
                        if EV == "cp":
                            av = p.do("scalar", lambda e, ps=ps: e.copy(
                                c.b32(o_sp + ps * 2048, 128, 512),
                                c.P32[:, A_PP[ps]:A_PP[ps] + 512]),
                                waits=[(c.s_pe[1], pv), tk["sp_free"][ps]],
                                inc=c.s_ac[1])
                            tk["sp_free"][ps] = (c.s_ac[1], av)
                            tk["pp_free"][ps] = (c.s_ac[1], av)
                            last = None
                            continue
                        if EV == "exp":
                            av = p.do("scalar", lambda e, ps=ps, d=d, boff=boff: e.activation(
                                c.b32(o_sp + ps * 2048, 128, 512),
                                c.P32[:, A_PP[ps]:A_PP[ps] + 512], AF.Exp,
                                bias=c.A32[:, boff // 4 + d: boff // 4 + d + 1]),
                                waits=[(c.s_pe[1], pv), tk["sp_free"][ps]],
                                inc=c.s_ac[1])
                            tk["sp_free"][ps] = (c.s_ac[1], av)
                            tk["pp_free"][ps] = (c.s_ac[1], av)
                            last = None
                            continue
                        p.do("scalar", lambda e, ps=ps, d=d, boff=boff: e.activation(
                            c.b32(o_sp + ps * 2048, 128, 512),
                            c.P32[:, A_PP[ps]:A_PP[ps] + 512], AF.Exp,
                            bias=c.A32[:, boff // 4 + d: boff // 4 + d + 1]),
                            waits=[(c.s_pe[1], pv), tk["sp_free"][ps]])
                        av = p.do("scalar", lambda e, ps=ps, d=d, c2=c2, dsto=dsto: e.activation(
                            c.b16(dsto + d * 2048 + c2 * 1024, 128, 512),
                            c.b32(o_sp + ps * 2048, 128, 512), AF.Ln, bias=1.0),
                            inc=c.s_ac[1])
                        tk["sp_free"][ps] = (c.s_ac[1], av)
                        tk["pp_free"][ps] = (c.s_ac[1], av)
                        last = av
                if last is not None:
                    tk["q_done" if which == "q" else "kfm_done"] = (c.s_ac[1], last)
        if PHX == "projq":
            return
        uslot = [0]
        for tt in range(8):
            for n in range(2):
                ps = uslot[0] % 2
                uslot[0] += 1
                wts0 = [(c.s_ac[0], ev_y_q[tt]), lw["k"], tk["pp_free"][ps],
                        tk["pn_free"][0], tk["pn_free"][1],
                        tk["pr_free"][0], tk["pr_free"][1]]
                for k in range(KT):
                    p.do("tensor", lambda e, tt=tt, n=n, k=k, ps=ps: e.matmul(
                        c.P32[:, A_PP[ps]:A_PP[ps] + 512],
                        yq(k, tt * 128, 128), wmat(1, k, n * 512, 512),
                        start=(k == 0), stop=False),
                        waits=wts0 if k == 0 else [])
                pv = p.do("tensor", lambda e, n=n, ps=ps: e.matmul(
                    c.P32[:, A_PP[ps]:A_PP[ps] + 512], c.ones,
                    c.row16(0, n * 512, 512), start=False, stop=True),
                    inc=c.s_pe[1])
                p.do("scalar", lambda e, ps=ps: e.activation(
                    c.b32(o_sp + ps * 2048, 128, 512),
                    c.P32[:, A_PP[ps]:A_PP[ps] + 512], AF.Exp),
                    waits=[(c.s_pe[1], pv), tk["sp_free"][ps]])
                av = p.do("scalar", lambda e, tt=tt, n=n, ps=ps: e.activation(
                    c.b16(o_ktm + tt * 2048 + n * 1024, 128, 512),
                    c.b32(o_sp + ps * 2048, 128, 512), AF.Ln, bias=1.0),
                    inc=c.s_ac[1])
                tk["sp_free"][ps] = (c.s_ac[1], av)
                if prefix:
                    dvm = p.do("vector", lambda e, tt=tt, n=n: e.tensor_scalar(
                        c.b16(o_ktm + tt * 2048 + n * 1024, 128, 512),
                        c.b16(o_ktm + tt * 2048 + n * 1024, 128, 512),
                        c.pmask, None, ALU.mult),
                        waits=[(c.s_ac[1], av)], inc=c.s_dv[1])
                    tk["ktm_done"] = (c.s_dv[1], dvm)
                else:
                    tk["ktm_done"] = (c.s_ac[1], av)
                tk["pp_free"][ps] = tk["ktm_done"]
            if PHX == "projk":
                continue
            for n in range(2):
                ps = uslot[0] % 2
                uslot[0] += 1
                wtsv = [(c.s_ac[0], ev_y_q[tt]), lw["v"], tk["pp_free"][ps]]
                for k in range(KT):
                    p.do("tensor", lambda e, tt=tt, n=n, k=k, ps=ps: e.matmul(
                        c.P32[:, A_PP[ps]:A_PP[ps] + 512],
                        yq(k, tt * 128, 128), wmat(2, k, n * 512, 512),
                        start=(k == 0), stop=False),
                        waits=wtsv if k == 0 else [])
                pv = p.do("tensor", lambda e, n=n, ps=ps: e.matmul(
                    c.P32[:, A_PP[ps]:A_PP[ps] + 512], c.ones,
                    c.row16(1, n * 512, 512), start=False, stop=True),
                    inc=c.s_pe[1])
                vpr = c.A16[:, (o_vp // 2) + tt * 1040: (o_vp // 2) + (tt + 1) * 1040].rearrange(
                    "p (h z) -> p h z", z=65)
                dv = p.do("vector", lambda e, n=n, ps=ps, vpr=vpr: e.tensor_copy(
                    vpr[:, n * 8:(n + 1) * 8, 0:64],
                    c.P32[:, A_PP[ps]:A_PP[ps] + 512].rearrange(
                        "p (h z) -> p h z", z=64)),
                    waits=[(c.s_pe[1], pv)], inc=c.s_dv[1])
                tk["pp_free"][ps] = (c.s_dv[1], dv)
                tk["v_done"] = (c.s_dv[1], dv)

    def state_mm(cc, hh, waits, defer=True):
        g = hh // 2
        so = A_PS[g % 2] + 65 * (hh % 2)
        vp = c.A16[:, (o_vp // 2) + cc * 1040 + 65 * hh:
                   (o_vp // 2) + cc * 1040 + 65 * hh + 65]
        kt = c.A16[:, (o_ktm // 2) + cc * 1024 + 64 * hh:
                   (o_ktm // 2) + cc * 1024 + 64 * hh + 64]
        p.do("tensor", lambda e, kt=kt, vp=vp, so=so: e.matmul(
            c.P32[0:64, so:so + 65], kt, vp, start=True, stop=True,
            tile_position=(0, 0)), waits=waits)
        pst = p.do("tensor", lambda e, kt=kt, vp=vp, so=so: e.matmul(
            c.P32[64:128, so:so + 65], kt, vp, start=True,
            stop=True, tile_position=(0, 64)), inc=c.s_pe[2])
        if hh % 2 == 0:
            return None
        h0 = hh - 1
        so0 = A_PS[g % 2]
        if first_state[0]:
            dv = p.do("vector", lambda e, h0=h0, so0=so0: e.tensor_copy(
                c.b32(o_S32 + 65 * h0 * 4, 128, 130),
                c.P32[:, so0:so0 + 130]),
                waits=[(c.s_pe[2], pst)], inc=c.s_dv[1])
        else:
            dv = p.do("vector", lambda e, h0=h0, so0=so0: e.tensor_add(
                c.b32(o_S32 + 65 * h0 * 4, 128, 130),
                c.b32(o_S32 + 65 * h0 * 4, 128, 130),
                c.P32[:, so0:so0 + 130]),
                waits=[(c.s_pe[2], pst)], inc=c.s_dv[1])
        return dv

    def scan_chunk(cc):
        def emit_A(h):
            sl = h % 2
            base = 64 * (h % 2)
            dtile = h // 2
            kf = c.A16[base:base + 64, (o_kfm // 2) + dtile * 1024 + cc * 128:
                       (o_kfm // 2) + dtile * 1024 + cc * 128 + 128]
            qf = c.A16[base:base + 64, (o_q // 2) + dtile * 1024 + cc * 128:
                       (o_q // 2) + dtile * 1024 + cc * 128 + 128]
            pv = p.do("tensor", lambda e, kf=kf, qf=qf, sl=sl: e.matmul(
                c.P32[:, A_PA[sl]:A_PA[sl] + 128], kf, qf, start=True, stop=True),
                waits=[tk["q_done"], tk["kfm_done"], tk["pa_free"][sl]],
                inc=c.s_pe[2])
            atm = c.b16(o_atm + sl * 256, 128, 128)
            dv = p.do("vector", lambda e, sl=sl, atm=atm: e.tensor_tensor(
                atm, c.P32[:, A_PA[sl]:A_PA[sl] + 128], c.trimask, ALU.mult),
                waits=[(c.s_pe[2], pv), tk["atm_free"][sl]], inc=c.s_dv[1])
            tk["pa_free"][sl] = (c.s_dv[1], dv)
            return (c.s_dv[1], dv), atm

        pend = None
        last_ev = None
        for h in range(17):
            cur = emit_A(h) if h < 16 else None
            if pend is not None:
                hh, maskev, atm = pend
                sl = hh % 2
                base = 64 * (hh % 2)
                dtile = hh // 2
                vp = c.A16[:, (o_vp // 2) + cc * 1040 + 65 * hh:
                           (o_vp // 2) + cc * 1040 + 65 * hh + 65]
                qf = c.A16[base:base + 64, (o_q // 2) + dtile * 1024 + cc * 128:
                           (o_q // 2) + dtile * 1024 + cc * 128 + 128]
                sb = c.A16[base:base + 64, (o_S16 // 2) + 65 * hh:
                           (o_S16 // 2) + 65 * hh + 65]
                p.do("tensor", lambda e, atm=atm, vp=vp, sl=sl: e.matmul(
                    c.P32[:, A_PN[sl]:A_PN[sl] + 65], atm, vp,
                    start=True, stop=False),
                    waits=[maskev, tk["v_done"], tk["pn_free"][sl],
                           tk["Sbf"][hh]])
                pnum = p.do("tensor", lambda e, qf=qf, sb=sb, sl=sl: e.matmul(
                    c.P32[:, A_PN[sl]:A_PN[sl] + 65], qf, sb,
                    start=False, stop=True), inc=c.s_pe[2])
                w = ([tk["ktm_done"], tk["ps_free"][(hh // 2) % 2]]
                     if hh % 2 == 0 else [])
                dvst = state_mm(cc, hh, w)
                rec = c.b32(o_rec + sl * 64, 128, 1)
                dvr = p.do("vector", lambda e, rec=rec, sl=sl: e.reciprocal(
                    rec, c.P32[:, A_PN[sl] + 64:A_PN[sl] + 65]),
                    waits=[(c.s_pe[2], pnum), tk["rec_free"][sl]],
                    inc=c.s_dv[1])
                aev = p.do("scalar", lambda e, hh=hh, rec=rec, sl=sl: e.activation(
                    c.b16(o_out + cc * 2048 + 64 * hh * 2, 128, 64),
                    c.P32[:, A_PN[sl]:A_PN[sl] + 64], AF.Copy, scale=rec),
                    waits=[(c.s_dv[1], dvr)], inc=c.s_ac[2])
                tk["rec_free"][sl] = (c.s_ac[2], aev)
                tk["pn_free"][sl] = (c.s_ac[2], aev)
                tk["atm_free"][sl] = (c.s_pe[2], pnum)
                last_ev = aev
                if hh % 2 == 1:
                    h0 = hh - 1
                    dsb = p.do("vector", lambda e, h0=h0: e.tensor_copy(
                        c.b16(o_S16 + 65 * h0 * 2, 128, 130),
                        c.b32(o_S32 + 65 * h0 * 4, 128, 130)), inc=c.s_dv[1])
                    tk["Sbf"][h0] = (c.s_dv[1], dsb)
                    tk["Sbf"][h0 + 1] = (c.s_dv[1], dsb)
                    tk["ps_free"][(hh // 2) % 2] = (c.s_dv[1], dsb)
            pend = (h, cur[0], cur[1]) if h < 16 else None
        tk["out_ev"][cc] = (c.s_ac[2], last_ev)

    def wo_quarter(qi):
        for cc in range(8):
            gt = qi * 8 + cc
            sl = cc % 2
            evx = c.dq_x.issue(lambda e, gt=gt, sl=sl: e.dma_start(
                out=c.b32(o_x2 + sl * 4096, 128, 1024),
                in_=xin[gt * 128:(gt + 1) * 128, :]),
                waits=[tk["x2_free"][sl]])
            for j in range(KT):
                ts = A_TRO[j % 2]
                wts = [tk["out_ev"][cc], tk["tro_free"][j % 2],
                       tk["pa_free"][0], tk["pa_free"][1]]
                pv = p.do("tensor", lambda e, j=j, ts=ts, cc=cc: e.transpose(
                    c.P16[:, ts:ts + 128],
                    c.b16(o_out + cc * 2048 + j * 256, 128, 128), c.ident),
                    waits=wts, inc=c.s_pe[3])
                av = p.do("scalar", lambda e, j=j, ts=ts, sl=sl: e.copy(
                    c.b16(o_ofm + sl * 2048 + j * 256, 128, 128),
                    c.P16[:, ts:ts + 128]),
                    waits=[(c.s_pe[3], pv)], inc=c.s_ac[2])
                tk["tro_free"][j % 2] = (c.s_ac[2], av)
            aofm = tk["tro_free"][(KT - 1) % 2]
            dvn = None
            for n in range(2):
                wts = [aofm, tk["pr_free"][n], tk["pn_free"][n],
                       tk["ps_free"][0], tk["ps_free"][1]]
                for j in range(KT):
                    p.do("tensor", lambda e, j=j, n=n, sl=sl: e.matmul(
                        c.P32[:, A_PR[n]:A_PR[n] + 512],
                        c.b16(o_ofm + sl * 2048 + j * 256, 128, 128),
                        wmat(3, j, n * 512, 512), start=(j == 0), stop=False),
                        waits=wts if j == 0 else [])
                pv = p.do("tensor", lambda e, n=n: e.matmul(
                    c.P32[:, A_PR[n]:A_PR[n] + 512], c.ones,
                    c.row16(2, n * 512, 512), start=False, stop=True),
                    inc=c.s_pe[3])
                dvn = p.do("vector", lambda e, n=n, sl=sl: e.tensor_add(
                    c.b32(o_nx + sl * 4096 + n * 2048, 128, 512),
                    c.b32(o_x2 + sl * 4096 + n * 2048, 128, 512),
                    c.P32[:, A_PR[n]:A_PR[n] + 512]),
                    waits=[(c.s_pe[3], pv), evx, tk["nx_free"][sl]],
                    inc=c.s_dv[2])
                tk["pr_free"][n] = (c.s_dv[2], dvn)
            stv = c.dq_st.issue(lambda e, gt=gt, sl=sl: e.dma_start(
                out=xout[gt * 128:(gt + 1) * 128, :],
                in_=c.b32(o_nx + sl * 4096, 128, 1024)),
                waits=[(c.s_dv[2], dvn)])
            tk["nx_free"][sl] = stv
            tk["x2_free"][sl] = (c.s_dv[2], dvn)

    PH = os.environ.get("KDBG_PH", "all")

    def quarter(qi, prefix):
        xsrc = c.x_pre if prefix else xin
        ev_y = {}
        if os.environ.get("KDBG_LN", "1") == "0":
            mv = p.do("vector", lambda e: e.memset(
                c.b16(o_y, 128, 8192), 0.001), inc=c.s_dv[0])
            acv = p.do("scalar", lambda e: e.copy(
                c.b16(o_y, 128, 2), c.b16(o_y, 128, 2)), inc=c.s_ac[0])
            for tt in range(8):
                ev_y[tt] = acv
            wy = (c.s_ac[0], acv)
            if PH == "ln":
                return
            proj_phase(qi, prefix, wy, ev_y)
            return
        for tt in range(8):
            gt = qi * 8 + tt
            evx = c.dq_x.issue(lambda e, gt=gt, tt=tt, xsrc=xsrc: e.dma_start(
                out=c.b32(o_x2 + (tt % 2) * 4096, 128, 1024),
                in_=xsrc[gt * 128:(gt + 1) * 128, :]),
                waits=[tk["x2_free"][tt % 2]])
            ydst = lambda j, tt=tt: yq(j, tt * 128, 128)
            ev_y[tt] = _ln_tile(c, c.b32(o_x2 + (tt % 2) * 4096, 128, 1024),
                                ydst, tt, [evx], A_TRLN, lnb)
            tk["x2_free"][tt % 2] = (c.s_ac[0], ev_y[tt])
        wy = (c.s_ac[0], ev_y[7])
        if PH == "ln":
            return
        proj_phase(qi, prefix, wy, ev_y)
        if PH.startswith("proj"):
            return
        if prefix:
            for cc in range(8):
                for hh in range(16):
                    g = hh // 2
                    w = ([tk["ktm_done"], tk["v_done"], tk["ps_free"][g % 2]]
                         if hh % 2 == 0 else [])
                    dv = state_mm(cc, hh, w)
                    if dv is not None:
                        tk["ps_free"][g % 2] = (c.s_dv[1], dv)
                        tk["Sadd"] = (c.s_dv[1], dv)
                if first_state[0]:
                    first_state[0] = False
        else:
            for cc in range(8):
                scan_chunk(cc)
            if PH == "scan":
                return
            wo_quarter(qi)

    NPRE = int(os.environ.get("KDBG_PRE", "4"))
    NOWN = int(os.environ.get("KDBG_OWN", "4"))
    for qi in range(NPRE):
        quarter(qi, prefix=True)
    if NPRE > 0 and PH == "all":
        sbv = p.do("vector", lambda e: e.tensor_copy(
            c.b16(o_S16, 128, 1040), c.b32(o_S32, 128, 1040)),
            waits=[tk["Sadd"]], inc=c.s_dv[1])
    else:
        p.do("vector", lambda e: e.memset(c.b32(o_S32, 128, 1040), 0.0))
        sbv = p.do("vector", lambda e: e.memset(c.b16(o_S16, 128, 1040), 0.0),
                   inc=c.s_dv[1])
        first_state[0] = False
    for h in range(16):
        tk["Sbf"][h] = (c.s_dv[1], sbv)
    for qi in range(NOWN):
        quarter(qi, prefix=False)


def _attn_b(c, xin, xout):
    p = c.p
    B = c.POOL
    o_y = B
    o_wv = o_y + 65536
    o_x2 = o_wv + 16384
    o_nx = o_x2 + 8192
    o_ln = o_nx + 8192
    assert o_ln + LNB_BYTES <= ARENA_BYTES

    lnb = _new_lnb(o_ln)
    yfm = lambda k, c0, n: c.b16(o_y + k * 8192 + c0 * 2, 128, n)
    PV = [0, 1024]
    TRLN = [6400, 7168]

    lwv = None
    for k in range(KT):
        lwv = c.dq_wl.issue(lambda e, k=k: e.dma_start(
            out=c.b16(o_wv + k * 2048, 128, 1024), in_=c.i_bw_vT[k]))

    tk = {"x2_free": [None, None], "nx_free": [None, None],
          "pv_free": [None, None]}
    ev_y = {}
    for tt in range(TILES):
        evx = c.dq_x.issue(lambda e, tt=tt: e.dma_start(
            out=c.b32(o_x2 + (tt % 2) * 4096, 128, 1024),
            in_=xin[tt * 128:(tt + 1) * 128, :]),
            waits=[tk["x2_free"][tt % 2]])
        ydst = lambda j, tt=tt: yfm(j, tt * 128, 128)
        ev_y[tt] = _ln_tile(c, c.b32(o_x2 + (tt % 2) * 4096, 128, 1024),
                            ydst, tt, [evx], TRLN, lnb)
        tk["x2_free"][tt % 2] = (c.s_ac[0], ev_y[tt])

    for tt in range(TILES):
        sl = tt % 2
        evx2 = c.dq_x.issue(lambda e, tt=tt, sl=sl: e.dma_start(
            out=c.b32(o_x2 + sl * 4096, 128, 1024),
            in_=xin[tt * 128:(tt + 1) * 128, :]),
            waits=[tk["x2_free"][sl]])
        for n in range(2):
            wts = [(c.s_ac[0], ev_y[tt]), lwv, tk["pv_free"][sl]]
            for k in range(KT):
                p.do("tensor", lambda e, tt=tt, n=n, k=k, sl=sl: e.matmul(
                    c.P32[:, PV[sl] + n * 512:PV[sl] + n * 512 + 512],
                    yfm(k, tt * 128, 128),
                    c.b16(o_wv + k * 2048 + n * 512 * 2, 128, 512),
                    start=(k == 0), stop=False),
                    waits=wts if (k == 0 and n == 0) else [])
            pv = p.do("tensor", lambda e, n=n, sl=sl: e.matmul(
                c.P32[:, PV[sl] + n * 512:PV[sl] + n * 512 + 512],
                c.ones, c.row16(3, n * 512, 512), start=False, stop=True),
                inc=c.s_pe[3] if n == 1 else None)
        dv = p.do("vector", lambda e, sl=sl: e.tensor_add(
            c.b32(o_nx + sl * 4096, 128, 1024),
            c.b32(o_x2 + sl * 4096, 128, 1024),
            c.P32[:, PV[sl]:PV[sl] + 1024]),
            waits=[(c.s_pe[3], pv), evx2, tk["nx_free"][sl]],
            inc=c.s_dv[2])
        tk["pv_free"][sl] = (c.s_dv[2], dv)
        stv = c.dq_st.issue(lambda e, tt=tt, sl=sl: e.dma_start(
            out=xout[tt * 128:(tt + 1) * 128, :],
            in_=c.b32(o_nx + sl * 4096, 128, 1024)),
            waits=[(c.s_dv[2], dv)])
        tk["nx_free"][sl] = stv
        tk["x2_free"][sl] = (c.s_dv[2], dv)


def _stage(inputs):
    f = np.ascontiguousarray
    s = {}
    s["trimask"] = f(np.triu(np.ones((128, 128), dtype=np.float32)))
    s["identb"] = np.eye(128, dtype=ml_dtypes.bfloat16)
    for nm, w in (("aw_qT", inputs["L0_attn_a_Wq"]), ("aw_kT", inputs["L0_attn_a_Wk"]),
                  ("aw_vT", inputs["L0_attn_a_Wv"]), ("aw_oT", inputs["L0_attn_a_Wo"]),
                  ("bw_vT", inputs["L0_attn_b_Wv"])):
        s[nm] = f(np.asarray(w).T.reshape(KT, 128, D))
    s["aw_bq"] = f(np.asarray(inputs["L0_attn_a_bq"]).reshape(KT, 128).T)
    s["aw_bk"] = f(np.asarray(inputs["L0_attn_a_bk"]).reshape(KT, 128).T)
    s["rows"] = f(np.stack([np.asarray(inputs["L0_attn_a_bk"]),
                            np.asarray(inputs["L0_attn_a_bv"]),
                            np.asarray(inputs["L0_attn_a_bo"]),
                            np.asarray(inputs["L0_attn_b_bv"]),
                            np.asarray(inputs["L0_ffn_s_fc2_b"])]).astype(np.float32).reshape(1, 5 * D))
    s["gateS"] = f(np.asarray(inputs["L0_ffn_gate_w"]).T.reshape(KT, 128, E))
    F1 = np.asarray(inputs["L0_ffn_fc1_w"]).transpose(0, 2, 1)
    s["fc1S"] = f(F1.reshape(E, KT, 128, 2, 8, 128).transpose(0, 4, 2, 1, 3, 5).reshape(E, 8, 128, 2048))
    B1 = np.asarray(inputs["L0_ffn_fc1_b"]).reshape(E, 2, 8, 128)
    s["fc1bS"] = f(B1.transpose(3, 0, 2, 1).reshape(128, E * 16))
    s["multS"] = f(np.asarray(inputs["L0_ffn_geglu_mult"]).reshape(E, 8, 128).transpose(2, 0, 1).reshape(128, E * 8))
    s["fc2S"] = f(np.asarray(inputs["L0_ffn_fc2_w"]).transpose(0, 2, 1).reshape(E, KT, 128, 1024))
    s["fc2bS"] = f(np.asarray(inputs["L0_ffn_fc2_b"]).astype(np.float32))
    S1 = np.asarray(inputs["L0_ffn_s_fc1_w"]).T
    s["sfc1S"] = f(S1.reshape(KT, 128, 2, 16, 128).transpose(3, 1, 0, 2, 4).reshape(16, 128, 2048))
    SB = np.asarray(inputs["L0_ffn_s_fc1_b"]).reshape(2, 16, 128)
    s["sfc1bS"] = f(SB.transpose(2, 1, 0).reshape(128, 32))
    s["smultS"] = f(np.asarray(inputs["L0_ffn_s_geglu_mult"]).reshape(16, 128).T)
    s["sfc2S"] = f(np.asarray(inputs["L0_ffn_s_fc2_w"]).T.reshape(16, 128, 1024))
    return s


def kernel(**inputs):
    x = np.asarray(inputs["L0_attn_a_x"], dtype=np.float32)
    shared = _stage(inputs)
    nc = build(NSUB)
    in_maps = []
    for core in range(8):
        b, h = core // 2, core % 2
        m = dict(shared)
        m["x_own"] = np.ascontiguousarray(x[b, h * NT:(h + 1) * NT, :])
        m["x_pre"] = np.ascontiguousarray(x[b, 0:NT, :])
        m["pmask"] = np.full((128, 1), float(h), dtype=np.float32)
        in_maps.append(m)
    res = run_bass_kernel_spmd(nc, in_maps, core_ids=list(range(8)))
    out = np.empty((4, 8192, D), dtype=np.float32)
    for core in range(8):
        b, h = core // 2, core % 2
        out[b, h * NT:(h + 1) * NT, :] = res.results[core]["y"]
    return out
